# version 03
# baseline (speedup 1.0000x reference)
import sys
for _p in ("/opt/trn_rl_repo",):
    if _p not in sys.path:
        sys.path.append(_p)
import numpy as np
import ml_dtypes
from contextlib import ExitStack

import concourse.bass as bass
import concourse.mybir as mybir
from concourse.bass_utils import run_bass_kernel_spmd

F32 = mybir.dt.float32
BF16 = mybir.dt.bfloat16
AF = mybir.ActivationFunctionType
ALU = mybir.AluOpType

NCORES = 8
T = 8192
TC = T // NCORES
D = 2048
NKV, QM, HD = 8, 4, 64
INTER = 5632
NIT = INTER // 128
NDT = D // 128
NCH = T // 128
LSPS = {0: [30, 31, 0, 1], 1: [0, 1, 2, 3]}

VE_CH = 20
PE_CH = 4
STK_CH = 48


class Sem:
    def __init__(self, ctx, nc, name):
        self.h = ctx.enter_context(nc.semaphore(name))
        self.n = 0


def build_nc(phases=("p0", "p1a", "p1b", "p2", "p3"), dbg=False):
    nc = bass.Bass("TRN2", target_bir_lowering=False)

    def din(name, shape, dt=F32):
        return nc.dram_tensor(name, list(shape), dt, kind="ExternalInput").ap()

    def dscr(name, shape, dt=BF16):
        kind = "ExternalOutput" if dbg else "Internal"
        return nc.dram_tensor(name, list(shape), dt, kind=kind).ap()

    x_c = din("x_c", [T, D])
    xT_c = din("xT_c", [D, TC])
    coss = din("coss", [128, T])
    sins = din("sins", [128, T])
    masks = din("masks", [128, 8 * 1024], BF16)
    wk_r = din("wk_r", [16, 128, 512])
    wv_r = din("wv_r", [16, 128, 512])
    wq_r = din("wq_r", [NKV, 16, 128, 512])
    wout_r = din("wout_r", [16, 128, 16, 128])
    gw_r = din("gw_r", [NIT, 128, 16, 128])
    uw_r = din("uw_r", [NIT, 128, 16, 128])
    dw_r = din("dw_r", [16, 128, NIT, 128])
    gsc = din("gsc", [128, NIT])
    usc = din("usc", [128, NIT])
    dsc = din("dsc", [128, 16])
    sink = din("sink", [1, 32])
    pswap = din("pswap", [128, 128], BF16)
    ident = din("ident", [128, 128], BF16)
    onesr = din("onesr", [1, 128])
    konst = din("konst", [128, 4])

    out_d = nc.dram_tensor("out_d", [D, TC], F32, kind="ExternalOutput").ap()

    kt_d = dscr("kt_d", [NKV, 128, 4096])
    v_d = dscr("v_d", [NKV, 128, 64 * 65])
    q_d = dscr("q_d", [NKV, 128, 4096])
    hdn_d = dscr("hdn_d", [NIT, 128, 1024])
    if dbg:
        xn_dbg = nc.dram_tensor("xn_dbg", [128, 8 * 2048], BF16, kind="ExternalOutput").ap()
        g_dbg = nc.dram_tensor("g_dbg", [128, 16 * 1024], BF16, kind="ExternalOutput").ap()
        ctx_dbg = nc.dram_tensor("ctx_dbg", [128, 16 * 1024], BF16, kind="ExternalOutput").ap()

    ctx = ExitStack()
    with ctx:
        sb = lambda n, s, d: ctx.enter_context(nc.sbuf_tensor(n, list(s), d))
        sem = lambda n: Sem(ctx, nc, n)
        block = ctx.enter_context(nc.Block())

        xn_T = sb("xn_T", [128, 8 * 2048], BF16)
        g_T = sb("g_T", [128, 16 * 1024], BF16)
        ctx_sb = sb("ctx_sb", [128, 16 * 1024], BF16)
        gsc_sb = sb("gsc_sb", [128, NIT], F32)
        usc_sb = sb("usc_sb", [128, NIT], F32)
        dsc_sb = sb("dsc_sb", [128, 16], F32)
        pswap_sb = sb("pswap_sb", [128, 128], BF16)
        ident_sb = sb("ident_sb", [128, 128], BF16)
        ones_sb = sb("ones_sb", [1, 128], F32)
        sink_sb = sb("sink_sb", [1, 32], F32)
        esink_sb = sb("esink_sb", [1, 32], F32)
        konst_sb = sb("konst_sb", [128, 4], F32)

        LDC = sem("LDC")

        @block.gpsimd
        def _(g):
            for dst, src in ((gsc_sb, gsc), (usc_sb, usc), (dsc_sb, dsc),
                             (pswap_sb, pswap), (ident_sb, ident),
                             (ones_sb, onesr), (sink_sb, sink),
                             (konst_sb, konst)):
                g.dma_start(out=dst[:], in_=src).then_inc(LDC.h, 16)
                LDC.n += 16

        xnr = xn_T[:].rearrange("p (ct r) -> p ct r", ct=8)

        def xn_rhs(dt, n2):
            return xnr[:, 4 * n2:4 * n2 + 4, dt * 128:(dt + 1) * 128]

        p0_final = {}
        if "p0" in phases:
            with ExitStack() as p0:
                sb0 = lambda n, s, d: p0.enter_context(nc.sbuf_tensor(n, list(s), d))
                ps0 = lambda n, s, d: p0.enter_context(nc.psum_tensor(n, list(s), d))
                wk_sb = sb0("wk_sb", [128, 16 * 512], BF16)
                wv_sb = sb0("wv_sb", [128, 16 * 512], BF16)
                xch = [sb0(f"xch{i}", [128, 2048], F32) for i in range(2)]
                junk = sb0("junk0", [128, 2048], F32)
                junk2 = sb0("junk2", [128, 2048], F32)
                stat = sb0("stat", [128, 8], F32)
                xnm = sb0("xnm", [128, 2048], BF16)
                xnchT = [sb0(f"xnchT{i}", [128, 2048], BF16) for i in range(2)]
                cosl = sb0("cosl", [128, 2048], F32)
                sinl = sb0("sinl", [128, 2048], F32)
                kbf = [sb0(f"kbf{i}", [128, 512], BF16) for i in range(2)]
                t1s = sb0("t1s", [128, 512], F32)
                t2s = sb0("t2s", [128, 512], F32)
                krope = [sb0(f"krope{i}", [128, 512], BF16) for i in range(2)]
                vbf = [sb0(f"vbf{i}", [128, 8 * 65], BF16) for i in range(2)]
                kps = ps0("kps", [128, 512], F32)
                vps = ps0("vps", [128, 512], F32)
                ksw = ps0("ksw", [128, 512], F32)
                xnT_ps = ps0("xnT_ps", [128, 2048], BF16)

                LDW0, LDR = sem("LDW0"), sem("LDR")
                LDX = [sem("LDX0"), sem("LDX1")]
                VE0, PE0, SE0 = sem("VE0"), sem("PE0"), sem("SE0")
                STK = [sem("STK0"), sem("STK1")]

                @block.gpsimd
                def _(g):
                    for i in range(4):
                        dst = wk_sb[:, i * 2048:(i + 1) * 2048].rearrange(
                            "p (dt c) -> p dt c", dt=4)
                        g.dma_start(out=dst,
                                    in_=wk_r[i * 4:(i + 1) * 4].rearrange(
                                        "dt p c -> p dt c")).then_inc(LDW0.h, 16)
                        LDW0.n += 16
                        dst = wv_sb[:, i * 2048:(i + 1) * 2048].rearrange(
                            "p (dt c) -> p dt c", dt=4)
                        g.dma_start(out=dst,
                                    in_=wv_r[i * 4:(i + 1) * 4].rearrange(
                                        "dt p c -> p dt c")).then_inc(LDW0.h, 16)
                        LDW0.n += 16

                @block.sync
                def _(sy):
                    for ct in range(NCH):
                        if ct % 16 == 0:
                            if ct > 0:
                                sy.wait_ge(VE0.h, (ct - 1) * VE_CH + 19)
                            sy.dma_start(out=cosl[:], in_=coss[:, ct * 128:(ct + 16) * 128]).then_inc(LDR.h, 16)
                            LDR.n += 16
                            sy.dma_start(out=sinl[:], in_=sins[:, ct * 128:(ct + 16) * 128]).then_inc(LDR.h, 16)
                            LDR.n += 16
                        if ct >= 2:
                            sy.wait_ge(VE0.h, (ct - 2) * VE_CH + 7)
                        sy.dma_start(out=xch[ct % 2][:],
                                     in_=x_c[ct * 128:(ct + 1) * 128, :]).then_inc(LDX[ct % 2].h, 16)
                        LDX[ct % 2].n += 16

                @block.scalar
                def _(s):
                    s.wait_ge(LDC.h, LDC.n)
                    for ct in range(NCH):
                        s.wait_ge(VE0.h, ct * 20 + 5)
                        ins = s.activation(stat[:, 7:8], stat[:, 4:5], AF.Ln,
                                           bias=konst_sb[:, 0:1])
                        ins.then_inc(SE0.h, 1); SE0.n += 1
                        s.wait_ge(SE0.h, ct * 2 + 1)
                        ins = s.activation(stat[:, 5:6], stat[:, 7:8], AF.Exp,
                                           scale=konst_sb[:, 1:2])
                        ins.then_inc(SE0.h, 1); SE0.n += 1

                @block.vector
                def _(v):
                    def vdo(ins):
                        ins.then_inc(VE0.h, 1); VE0.n += 1
                        return ins
                    v.memset(vbf[0][:].rearrange("p (h c) -> p h c", h=8)[:, :, 64:65], 1.0)
                    v.memset(vbf[1][:].rearrange("p (h c) -> p h c", h=8)[:, :, 64:65], 1.0)
                    for ct in range(NCH):
                        xc = xch[ct % 2]
                        v.wait_ge(LDX[ct % 2].h, (ct // 2 + 1) * 16)
                        if ct % 16 == 0:
                            v.wait_ge(LDR.h, (ct // 16 + 1) * 32)
                        vdo(v.tensor_scalar(junk[:], xc[:], 1.0, 0.0, op0=ALU.mult,
                                            op1=ALU.add, accum_out=stat[:, 0:1]))
                        vdo(v.scalar_tensor_tensor(junk2[:], xc[:], 1.0, xc[:], op0=ALU.mult,
                                                   op1=ALU.mult, accum_out=stat[:, 1:2]))
                        v.wait_ge(VE0.h, VE0.n)
                        vdo(v.tensor_scalar(stat[:, 2:3], stat[:, 0:1], 1.0 / D, None, op0=ALU.mult))
                        v.wait_ge(VE0.h, VE0.n)
                        vdo(v.tensor_tensor(stat[:, 3:4], stat[:, 2:3], stat[:, 2:3], op=ALU.mult))
                        v.wait_ge(VE0.h, VE0.n)
                        vdo(v.scalar_tensor_tensor(stat[:, 4:5], stat[:, 1:2], 1.0 / D,
                                                   stat[:, 3:4], op0=ALU.mult, op1=ALU.subtract))
                        v.wait_ge(SE0.h, (ct + 1) * 2)
                        vdo(v.tensor_scalar(stat[:, 6:7], stat[:, 2:3], stat[:, 5:6], -1.0,
                                            op0=ALU.mult, op1=ALU.mult))
                        v.wait_ge(VE0.h, VE0.n)
                        vdo(v.tensor_scalar(xnm[:], xc[:], stat[:, 5:6], stat[:, 6:7],
                                            op0=ALU.mult, op1=ALU.add))
                        v.wait_ge(PE0.h, ct * PE_CH + 1)
                        if ct >= 2:
                            v.wait_ge(PE0.h, (ct - 2) * PE_CH + 3)
                        vdo(v.tensor_copy(xnchT[ct % 2][:], xnT_ps[:]))
                        if ct < 8:
                            vdo(v.tensor_copy(xn_T[:, ct * 2048:(ct + 1) * 2048], xnT_ps[:]))
                        else:
                            vdo(v.tensor_copy(junk[:, 0:8], xnT_ps[:, 0:8]))
                        v.wait_ge(PE0.h, ct * PE_CH + 2)
                        if ct >= 2:
                            v.wait_ge(STK[ct % 2].h, (ct // 2) * STK_CH)
                        vdo(v.tensor_copy(kbf[ct % 2][:], kps[:]))
                        v.wait_ge(PE0.h, ct * PE_CH + 4)
                        co = cosl[:, (ct % 16) * 128:(ct % 16 + 1) * 128]
                        si = sinl[:, (ct % 16) * 128:(ct % 16 + 1) * 128]
                        for b in range(4):
                            sl = slice(b * 128, (b + 1) * 128)
                            vdo(v.tensor_tensor(t1s[:, sl], ksw[:, sl], si, op=ALU.mult))
                            vdo(v.tensor_tensor(t2s[:, sl], kps[:, sl], co, op=ALU.mult))
                        v.wait_ge(VE0.h, VE0.n)
                        vdo(v.tensor_tensor(krope[ct % 2][:], t1s[:], t2s[:], op=ALU.add))
                        v.wait_ge(PE0.h, ct * PE_CH + 3)
                        vdo(v.tensor_copy(
                            vbf[ct % 2][:].rearrange("p (h c) -> p h c", h=8)[:, :, 0:64],
                            vps[:].rearrange("p (h c) -> p h c", h=8)))

                @block.tensor
                def _(t):
                    t.wait_ge(LDW0.h, 8 * 16)
                    t.wait_ge(LDC.h, LDC.n)
                    for ct in range(NCH):
                        t.wait_ge(VE0.h, ct * VE_CH + 7)
                        for dt in range(NDT):
                            ins = t.transpose(xnT_ps[:, dt * 128:(dt + 1) * 128],
                                              xnm[:, dt * 128:(dt + 1) * 128], ident_sb[:])
                        ins.then_inc(PE0.h, 1); PE0.n += 1
                        t.wait_ge(VE0.h, ct * VE_CH + 8)
                        if ct >= 1:
                            t.wait_ge(VE0.h, (ct - 1) * VE_CH + 18)
                        xnc = xnchT[ct % 2]
                        for kt4 in range(4):
                            for dt in range(NDT):
                                ins = t.matmul(kps[:, kt4 * 128:(kt4 + 1) * 128],
                                               wk_sb[:, dt * 512 + kt4 * 128:dt * 512 + (kt4 + 1) * 128],
                                               xnc[:, dt * 128:(dt + 1) * 128],
                                               start=(dt == 0), stop=(dt == NDT - 1))
                        ins.then_inc(PE0.h, 1); PE0.n += 1
                        if ct >= 1:
                            t.wait_ge(VE0.h, (ct - 1) * VE_CH + 20)
                        for dt in range(NDT):
                            ins = t.matmul(vps[:],
                                           xnc[:, dt * 128:(dt + 1) * 128],
                                           wv_sb[:, dt * 512:(dt + 1) * 512],
                                           start=(dt == 0), stop=(dt == NDT - 1))
                        ins.then_inc(PE0.h, 1); PE0.n += 1
                        t.wait_ge(VE0.h, ct * VE_CH + 10)
                        ins = t.matmul(ksw[:], pswap_sb[:], kbf[ct % 2][:], start=True, stop=True)
                        ins.then_inc(PE0.h, 1); PE0.n += 1

                kt_dr = kt_d.rearrange("(kt4 hh) p k -> hh p kt4 k", hh=2)
                v_dr = v_d.rearrange("h k (ls c) -> ls k h c", c=65)

                @block.gpsimd
                def _(g):
                    for ct in range(NCH):
                        g.wait_ge(VE0.h, ct * VE_CH + 19)
                        kr = krope[ct % 2]
                        for hh in range(2):
                            dst = kt_dr[hh, (ct % 2) * 64:(ct % 2) * 64 + 64, :,
                                        (ct // 2) * 128:(ct // 2) * 128 + 128]
                            g.dma_start(out=dst,
                                        in_=kr[hh * 64:(hh + 1) * 64, :].rearrange(
                                            "d (kt4 k) -> d kt4 k", kt4=4)).then_inc(STK[ct % 2].h, 16)
                            STK[ct % 2].n += 16
                        g.wait_ge(VE0.h, ct * VE_CH + 20)
                        g.dma_start(out=v_dr[ct],
                                    in_=vbf[ct % 2][:].rearrange("k (h c) -> k h c", h=8)).then_inc(STK[ct % 2].h, 16)
                        STK[ct % 2].n += 16

                p0_final = dict(VE0=(VE0.h, VE0.n), PE0=(PE0.h, PE0.n),
                                SE0=(SE0.h, SE0.n), STK0=(STK[0].h, STK[0].n),
                                STK1=(STK[1].h, STK[1].n))

        p1a_final = {}
        if "p1a" in phases:
            with ExitStack() as p1:
                sb1 = lambda n, s, d: p1.enter_context(nc.sbuf_tensor(n, list(s), d))
                ps1 = lambda n, s, d: p1.enter_context(nc.psum_tensor(n, list(s), d))
                gw_sb = [sb1(f"gw{i}", [128, 2048], BF16) for i in range(2)]
                uw_sb = [sb1(f"uw{i}", [128, 2048], BF16) for i in range(2)]
                sil = [sb1(f"sil{i}", [128, 1024], F32) for i in range(2)]
                hdnst = [sb1(f"hdnst{i}", [128, 1024], BF16) for i in range(2)]
                g_ps = [ps1(f"g_ps{i}", [128, 1024], F32) for i in range(2)]
                u_ps = [ps1(f"u_ps{i}", [128, 1024], F32) for i in range(2)]

                LD1 = [sem("LD1a"), sem("LD1b")]
                STH = [sem("STHa"), sem("STHb")]
                PE1, VE1, SE1 = sem("PE1"), sem("VE1"), sem("SE1")

                def p1a_store(g, it):
                    g.wait_ge(VE1.h, it + 1)
                    g.dma_start(out=hdn_d[it], in_=hdnst[it % 2][:]).then_inc(STH[it % 2].h, 16)
                    STH[it % 2].n += 16

                @block.gpsimd
                def _(g):
                    for hs, vv in p0_final.values():
                        g.wait_ge(hs, vv)
                    for it in range(NIT):
                        if it >= 2:
                            g.wait_ge(PE1.h, (it - 1) * 2)
                        g.dma_start(out=gw_sb[it % 2][:].rearrange("p (dt c) -> p dt c", dt=16),
                                    in_=gw_r[it]).then_inc(LD1[it % 2].h, 16)
                        LD1[it % 2].n += 16
                        g.dma_start(out=uw_sb[it % 2][:].rearrange("p (dt c) -> p dt c", dt=16),
                                    in_=uw_r[it]).then_inc(LD1[it % 2].h, 16)
                        LD1[it % 2].n += 16
                        if it >= 2:
                            p1a_store(g, it - 2)
                    p1a_store(g, NIT - 2)
                    p1a_store(g, NIT - 1)

                @block.tensor
                def _(t):
                    for hs, vv in p0_final.values():
                        t.wait_ge(hs, vv)
                    for it in range(NIT):
                        t.wait_ge(LD1[it % 2].h, (it // 2 + 1) * 32)
                        if it >= 2:
                            t.wait_ge(SE1.h, it - 1)
                        for dt in range(NDT):
                            for n2 in range(2):
                                ins = t.matmul(g_ps[it % 2][:, n2 * 512:(n2 + 1) * 512],
                                               gw_sb[it % 2][:, dt * 128:(dt + 1) * 128],
                                               xn_rhs(dt, n2),
                                               start=(dt == 0), stop=(dt == NDT - 1))
                        ins.then_inc(PE1.h, 1); PE1.n += 1
                        if it >= 2:
                            t.wait_ge(VE1.h, it - 1)
                        for dt in range(NDT):
                            for n2 in range(2):
                                ins = t.matmul(u_ps[it % 2][:, n2 * 512:(n2 + 1) * 512],
                                               uw_sb[it % 2][:, dt * 128:(dt + 1) * 128],
                                               xn_rhs(dt, n2),
                                               start=(dt == 0), stop=(dt == NDT - 1))
                        ins.then_inc(PE1.h, 1); PE1.n += 1

                @block.scalar
                def _(s):
                    s.wait_ge(LDC.h, LDC.n)
                    for it in range(NIT):
                        s.wait_ge(PE1.h, it * 2 + 1)
                        if it >= 2:
                            s.wait_ge(VE1.h, it - 1)
                        ins = s.activation(sil[it % 2][:], g_ps[it % 2][:], AF.Silu,
                                           scale=gsc_sb[:, it:it + 1])
                        ins.then_inc(SE1.h, 1); SE1.n += 1

                @block.vector
                def _(v):
                    for hs, vv in p0_final.values():
                        v.wait_ge(hs, vv)
                    for it in range(NIT):
                        v.wait_ge(PE1.h, it * 2 + 2)
                        v.wait_ge(SE1.h, it + 1)
                        if it >= 2:
                            v.wait_ge(STH[it % 2].h, (it // 2) * 16)
                        ins = v.scalar_tensor_tensor(hdnst[it % 2][:], u_ps[it % 2][:],
                                                     usc_sb[:, it:it + 1], sil[it % 2][:],
                                                     op0=ALU.mult, op1=ALU.mult)
                        ins.then_inc(VE1.h, 1); VE1.n += 1

                p1a_final = dict(PE1=(PE1.h, PE1.n), VE1=(VE1.h, VE1.n),
                                 SE1=(SE1.h, SE1.n), STHa=(STH[0].h, STH[0].n),
                                 STHb=(STH[1].h, STH[1].n))

        p1b_final = {}
        if "p1b" in phases:
            with ExitStack() as p1b:
                sb2 = lambda n, s, d: p1b.enter_context(nc.sbuf_tensor(n, list(s), d))
                ps2 = lambda n, s, d: p1b.enter_context(nc.psum_tensor(n, list(s), d))
                dw_sb = [sb2(f"dw{i}", [128, NIT * 128], BF16) for i in range(2)]
                hdn_half = sb2("hdn_half", [128, NIT * 512], BF16)
                d_ps = [ps2(f"d_ps{i}", [128, 512], F32) for i in range(2)]

                LD2 = [sem("LD2a"), sem("LD2b")]
                LDH = sem("LDH")
                PE2, SE2 = sem("PE2"), sem("SE2")
                hdn_r = hdn_d.rearrange("it p t -> p it t")

                @block.sync
                def _(sy):
                    for hs, vv in p1a_final.values():
                        sy.wait_ge(hs, vv)
                    for th in range(2):
                        if th == 1:
                            sy.wait_ge(PE2.h, 16)
                        src = hdn_r[:, :, th * 512:(th + 1) * 512]
                        for q in range(4):
                            sy.dma_start(out=hdn_half[:, q * 11 * 512:(q + 1) * 11 * 512]
                                         .rearrange("p (it t) -> p it t", it=11),
                                         in_=src[:, q * 11:(q + 1) * 11, :]).then_inc(LDH.h, 16)
                            LDH.n += 16

                @block.gpsimd
                def _(g):
                    for hs, vv in p1a_final.values():
                        g.wait_ge(hs, vv)
                    for it2 in range(32):
                        dmt = it2 % 16
                        if it2 >= 2:
                            g.wait_ge(PE2.h, it2 - 1)
                        g.dma_start(out=dw_sb[it2 % 2][:].rearrange("p (it c) -> p it c", it=NIT),
                                    in_=dw_r[dmt]).then_inc(LD2[it2 % 2].h, 16)
                        LD2[it2 % 2].n += 16

                @block.tensor
                def _(t):
                    for hs, vv in p1a_final.values():
                        t.wait_ge(hs, vv)
                    for it2 in range(32):
                        th, dmt = it2 // 16, it2 % 16
                        t.wait_ge(LD2[it2 % 2].h, (it2 // 2 + 1) * 16)
                        t.wait_ge(LDH.h, (th + 1) * 64)
                        if it2 >= 2:
                            t.wait_ge(SE2.h, it2 - 1)
                        for it in range(NIT):
                            ins = t.matmul(d_ps[it2 % 2][:],
                                           dw_sb[it2 % 2][:, it * 128:(it + 1) * 128],
                                           hdn_half[:, it * 512:(it + 1) * 512],
                                           start=(it == 0), stop=(it == NIT - 1))
                        ins.then_inc(PE2.h, 1); PE2.n += 1

                @block.scalar
                def _(s):
                    s.wait_ge(LDC.h, LDC.n)
                    for it2 in range(32):
                        th, dmt = it2 // 16, it2 % 16
                        s.wait_ge(PE2.h, it2 + 1)
                        ins = s.activation(g_T[:, dmt * 1024 + th * 512:dmt * 1024 + th * 512 + 512],
                                           d_ps[it2 % 2][:], AF.Silu, scale=dsc_sb[:, dmt:dmt + 1])
                        ins.then_inc(SE2.h, 1); SE2.n += 1

                p1b_final = dict(PE2=(PE2.h, PE2.n), SE2=(SE2.h, SE2.n))

        p2_final = {}
        if "p2" in phases:
            with ExitStack() as p2o:
                sb3 = lambda n, s, d: p2o.enter_context(nc.sbuf_tensor(n, list(s), d))
                masks_sb = sb3("masks_sb", [128, 8 * 1024], BF16)
                LDM = sem("LDM")
                SE3 = sem("SE3")

                @block.sync
                def _(sy):
                    for hs, vv in p1b_final.values():
                        sy.wait_ge(hs, vv)
                    sy.dma_start(out=masks_sb[:], in_=masks).then_inc(LDM.h, 16)
                    LDM.n += 16

                @block.scalar
                def _(s):
                    s.wait_ge(LDC.h, LDC.n)
                    ins = s.activation(esink_sb[:], sink_sb[:], AF.Exp)
                    ins.then_inc(SE3.h, 1); SE3.n += 1

                LQR, LDQW, STQ = sem("LQR"), sem("LDQW"), sem("STQ")
                PE3, VE3 = sem("PE3"), sem("VE3")

                with ExitStack() as p2p:
                    sbp = lambda n, s, d: p2p.enter_context(nc.sbuf_tensor(n, list(s), d))
                    psp = lambda n, s, d: p2p.enter_context(nc.psum_tensor(n, list(s), d))
                    wqh_sb = sbp("wqh_sb", [128, 16 * 512], BF16)
                    qbf = sbp("qbf", [128, 1024], BF16)
                    qstage = sbp("qstage", [128, 4096], BF16)
                    cosq = sbp("cosq", [128, 1024], F32)
                    sinq = sbp("sinq", [128, 1024], F32)
                    t1q = sbp("t1q", [128, 1024], F32)
                    t2q = sbp("t2q", [128, 1024], F32)
                    qp = psp("qp", [128, 1024], F32)
                    qsw = psp("qsw", [128, 1024], F32)

                    @block.sync
                    def _(sy):
                        sy.dma_start(out=cosq[:], in_=coss[:, 0:1024]).then_inc(LQR.h, 16)
                        sy.dma_start(out=sinq[:], in_=sins[:, 0:1024]).then_inc(LQR.h, 16)
                        LQR.n += 32

                    def p2p_store(g, h):
                        g.wait_ge(VE3.h, (h + 1) * 16)
                        g.dma_start(out=q_d[h], in_=qstage[:]).then_inc(STQ.h, 16)
                        STQ.n += 16

                    @block.gpsimd
                    def _(g):
                        for hs, vv in p1b_final.values():
                            g.wait_ge(hs, vv)
                        for h in range(NKV):
                            if h >= 1:
                                g.wait_ge(PE3.h, 8 * h)
                            g.dma_start(out=wqh_sb[:].rearrange("p (dt c) -> p dt c", dt=16),
                                        in_=wq_r[h].rearrange("dt p c -> p dt c")).then_inc(LDQW.h, 16)
                            LDQW.n += 16
                            if h >= 1:
                                p2p_store(g, h - 1)
                        p2p_store(g, NKV - 1)

                    @block.tensor
                    def _(t):
                        for hs, vv in p1b_final.values():
                            t.wait_ge(hs, vv)
                        for h in range(NKV):
                            t.wait_ge(LDQW.h, (h + 1) * 16)
                            for m in range(QM):
                                idx = h * 4 + m
                                if idx >= 1:
                                    t.wait_ge(VE3.h, idx * 4 - 1)
                                for dt in range(NDT):
                                    for n2 in range(2):
                                        ins = t.matmul(qp[:, n2 * 512:(n2 + 1) * 512],
                                                       wqh_sb[:, dt * 512 + m * 128:dt * 512 + (m + 1) * 128],
                                                       xn_rhs(dt, n2),
                                                       start=(dt == 0), stop=(dt == NDT - 1))
                                ins.then_inc(PE3.h, 1); PE3.n += 1
                                t.wait_ge(VE3.h, idx * 4 + 1)
                                t.matmul(qsw[:, 0:512], pswap_sb[:], qbf[:, 0:512],
                                         start=True, stop=True)
                                ins = t.matmul(qsw[:, 512:1024], pswap_sb[:], qbf[:, 512:1024],
                                               start=True, stop=True)
                                ins.then_inc(PE3.h, 1); PE3.n += 1

                    @block.vector
                    def _(v):
                        def vdo(ins):
                            ins.then_inc(VE3.h, 1); VE3.n += 1
                            return ins
                        v.wait_ge(LQR.h, 32)
                        for hs, vv in p1b_final.values():
                            v.wait_ge(hs, vv)
                        for h in range(NKV):
                            for m in range(QM):
                                idx = h * 4 + m
                                v.wait_ge(PE3.h, idx * 2 + 1)
                                if m == 0 and h >= 1:
                                    v.wait_ge(STQ.h, 16 * h)
                                vdo(v.tensor_copy(qbf[:], qp[:]))
                                v.wait_ge(PE3.h, idx * 2 + 2)
                                vdo(v.tensor_tensor(t1q[:], qsw[:], sinq[:], op=ALU.mult))
                                vdo(v.tensor_tensor(t2q[:], qp[:], cosq[:], op=ALU.mult))
                                v.wait_ge(VE3.h, VE3.n)
                                vdo(v.tensor_tensor(qstage[:, m * 1024:(m + 1) * 1024],
                                                    t1q[:], t2q[:], op=ALU.add))

                    pe3_fin, ve3_fin, stq_fin = PE3.n, VE3.n, STQ.n

                with ExitStack() as p2m:
                    sbm = lambda n, s, d: p2m.enter_context(nc.sbuf_tensor(n, list(s), d))
                    psm = lambda n, s, d: p2m.enter_context(nc.psum_tensor(n, list(s), d))
                    KT = [sbm(f"KT{i}", [128, 4096], BF16) for i in range(2)]
                    VS = [sbm(f"VS{i}", [128, 64 * 65], BF16) for i in range(2)]
                    QS = [sbm(f"QS{i}", [128, 4096], BF16) for i in range(2)]
                    ES = [sbm(f"ES{i}", [128, 1024], BF16) for i in range(8)]
                    rbc = sbm("rbc", [128, 2048], F32)
                    dsum = sbm("dsum", [1, 2048], F32)
                    scores = [psm(f"scores{i}", [128, 1024], F32) for i in range(2)]
                    ctxp = psm("ctxp", [65, 2048], F32)

                    LKV = [sem("LKVa"), sem("LKVb")]
                    QKs, EXs, WVs, MKs, BCs, VE4 = (sem("QKs"), sem("EXs"), sem("WVs"),
                                                    sem("MKs"), sem("BCs"), sem("VE4"))

                    def masked_slot(qc, lsp):
                        return LSPS[qc].index(lsp) if lsp in LSPS[qc] else None

                    mk_after = {}
                    _cnt = 0
                    for _h in range(NKV):
                        for _qc in range(2):
                            for _lsp in range(32):
                                for _m in range(QM):
                                    _n = ((_h * 2 + _qc) * 32 + _lsp) * 4 + _m
                                    if masked_slot(_qc, _lsp) is not None:
                                        _cnt += 1
                                    mk_after[_n] = _cnt

                    @block.sync
                    def _(sy):
                        sy.wait_ge(LDM.h, 16)
                        sy.wait_ge(PE3.h, pe3_fin)
                        sy.wait_ge(VE3.h, ve3_fin)
                        sy.wait_ge(STQ.h, stq_fin)
                        if "p0" in phases:
                            for kk in ("STK0", "STK1"):
                                hs, vv = p0_final[kk]
                                sy.wait_ge(hs, vv)
                        for h in range(NKV):
                            if h >= 2:
                                sy.wait_ge(QKs.h, (h - 1) * 256)
                                sy.wait_ge(WVs.h, (h - 1) * 64)
                            sy.dma_start(out=KT[h % 2][:], in_=kt_d[h]).then_inc(LKV[h % 2].h, 16)
                            sy.dma_start(out=VS[h % 2][:], in_=v_d[h]).then_inc(LKV[h % 2].h, 16)
                            sy.wait_ge(STQ.h, 16 * (h + 1))
                            sy.dma_start(out=QS[h % 2][:], in_=q_d[h]).then_inc(LKV[h % 2].h, 16)
                            LKV[h % 2].n += 48

                    def emit_wv_batch(t, h, L, lsp):
                        g_last = L * 128 + lsp * 4 + 3
                        if masked_slot(L % 2, lsp) is not None:
                            t.wait_ge(MKs.h, mk_after[g_last])
                        else:
                            t.wait_ge(EXs.h, g_last + 1)
                        if lsp == 0:
                            t.wait_ge(VE4.h, L * 11)
                        for par in range(2):
                            for m in range(QM):
                                g = L * 128 + lsp * 4 + m
                                ins = t.matmul(ctxp[:, m * 512:(m + 1) * 512],
                                               VS[h % 2][:, (2 * lsp + par) * 65:(2 * lsp + par) * 65 + 65],
                                               ES[g % 8][:, par * 512:(par + 1) * 512],
                                               start=(lsp == 0 and par == 0),
                                               stop=(lsp == 31 and par == 1))
                        ins.then_inc(WVs.h, 1); WVs.n += 1

                    @block.tensor
                    def _(t):
                        t.wait_ge(PE3.h, pe3_fin)
                        for h in range(NKV):
                            t.wait_ge(LKV[h % 2].h, (h // 2 + 1) * 48)
                            for qc in range(2):
                                L = h * 2 + qc
                                for lsp in range(32):
                                    for m in range(QM):
                                        n = L * 128 + lsp * 4 + m
                                        if n >= 2:
                                            t.wait_ge(EXs.h, n - 1)
                                        if lsp == 0 and m == 0 and L >= 1:
                                            t.wait_ge(VE4.h, (L - 1) * 11 + 7)
                                        qs = QS[h % 2]
                                        csl = slice(m * 1024 + qc * 512, m * 1024 + (qc + 1) * 512)
                                        t.matmul(scores[n % 2][:, 0:512],
                                                 KT[h % 2][0:64, lsp * 128:(lsp + 1) * 128],
                                                 qs[0:64, csl], start=True, stop=True,
                                                 tile_position=(0, 0))
                                        ins = t.matmul(scores[n % 2][:, 512:1024],
                                                       KT[h % 2][64:128, lsp * 128:(lsp + 1) * 128],
                                                       qs[64:128, csl], start=True, stop=True,
                                                       tile_position=(64, 0))
                                        ins.then_inc(QKs.h, 1); QKs.n += 1
                                        if m == 1 and lsp >= 1:
                                            emit_wv_batch(t, h, L, lsp - 1)
                                emit_wv_batch(t, h, L, 31)
                                t.wait_ge(VE4.h, L * 11 + 5)
                                t.wait_ge(EXs.h, (L + 1) * 128)
                                for half in range(2):
                                    for q2 in range(2):
                                        ins = t.matmul(scores[half][:, q2 * 512:(q2 + 1) * 512],
                                                       ones_sb[:],
                                                       dsum[0:1, half * 1024 + q2 * 512:half * 1024 + (q2 + 1) * 512],
                                                       start=True, stop=True)
                                    ins.then_inc(BCs.h, 1); BCs.n += 1

                    @block.scalar
                    def _(s):
                        s.wait_ge(LDC.h, LDC.n)
                        for n in range(NKV * 2 * 128):
                            s.wait_ge(QKs.h, n + 1)
                            B = n // 4
                            if B >= 2:
                                s.wait_ge(WVs.h, B - 1)
                            ins = s.activation(ES[n % 8][:], scores[n % 2][:],
                                               AF.Exp, scale=konst_sb[:, 2:3])
                            ins.then_inc(EXs.h, 1); EXs.n += 1

                    @block.vector
                    def _(v):
                        v.wait_ge(SE3.h, 1)
                        v.wait_ge(LDM.h, 16)
                        for h in range(NKV):
                            for qc in range(2):
                                L = h * 2 + qc
                                for lsp in range(32):
                                    slot = masked_slot(qc, lsp)
                                    if slot is None:
                                        continue
                                    for m in range(QM):
                                        n = L * 128 + lsp * 4 + m
                                        v.wait_ge(EXs.h, n + 1)
                                        sidx = qc * 4 + slot
                                        ins = v.tensor_tensor(ES[n % 8][:], ES[n % 8][:],
                                                              masks_sb[:, sidx * 1024:(sidx + 1) * 1024],
                                                              op=ALU.mult)
                                        ins.then_inc(MKs.h, 1); MKs.n += 1
                                def vdo4(ins):
                                    ins.then_inc(VE4.h, 1); VE4.n += 1
                                    return ins
                                v.wait_ge(WVs.h, (L + 1) * 32)
                                vdo4(v.tensor_copy(dsum[:], ctxp[64:65, :]))
                                v.wait_ge(VE4.h, VE4.n)
                                for m in range(QM):
                                    vdo4(v.tensor_scalar(dsum[0:1, m * 512:(m + 1) * 512],
                                                         dsum[0:1, m * 512:(m + 1) * 512],
                                                         esink_sb[0:1, h * 4 + m:h * 4 + m + 1],
                                                         None, op0=ALU.add))
                                v.wait_ge(BCs.h, L * 2 + 1)
                                vdo4(v.reciprocal(rbc[:, 0:1024], scores[0][:]))
                                v.wait_ge(BCs.h, L * 2 + 2)
                                vdo4(v.reciprocal(rbc[:, 1024:2048], scores[1][:]))
                                v.wait_ge(VE4.h, VE4.n)
                                for m in range(QM):
                                    dvt = (h * 4 + m) // 2
                                    r0 = (m % 2) * 64
                                    vdo4(v.tensor_tensor(
                                        ctx_sb[r0:r0 + 64, dvt * 1024 + qc * 512:dvt * 1024 + qc * 512 + 512],
                                        ctxp[0:64, m * 512:(m + 1) * 512],
                                        rbc[0:64, m * 512:(m + 1) * 512], op=ALU.mult))

                    p2_final = dict(QKs=(QKs.h, QKs.n), EXs=(EXs.h, EXs.n),
                                    WVs=(WVs.h, WVs.n), VE4=(VE4.h, VE4.n),
                                    BCs=(BCs.h, BCs.n), MKs=(MKs.h, MKs.n))

        if "p3" in phases:
            with ExitStack() as p3:
                sb4 = lambda n, s, d: p3.enter_context(nc.sbuf_tensor(n, list(s), d))
                ps4 = lambda n, s, d: p3.enter_context(nc.psum_tensor(n, list(s), d))
                wo_sb = [sb4(f"wo{i}", [128, 2048], BF16) for i in range(2)]
                xt_sb = [sb4(f"xt{i}", [128, 1024], F32) for i in range(2)]
                prod = sb4("prod", [128, 1024], F32)
                outst = [sb4(f"outst{i}", [128, 1024], F32) for i in range(2)]
                s_ps = [ps4(f"s_ps{i}", [128, 1024], F32) for i in range(2)]

                LD5 = [sem("LD5a"), sem("LD5b")]
                LX3 = [sem("LX3a"), sem("LX3b")]
                STO = [sem("STOa"), sem("STOb")]
                PE5, VE5 = sem("PE5"), sem("VE5")

                @block.gpsimd
                def _(g):
                    for hs, vv in p2_final.values():
                        g.wait_ge(hs, vv)
                    for dmt in range(16):
                        if dmt >= 2:
                            g.wait_ge(PE5.h, dmt - 1)
                        g.dma_start(out=wo_sb[dmt % 2][:].rearrange("p (dv c) -> p dv c", dv=16),
                                    in_=wout_r[dmt]).then_inc(LD5[dmt % 2].h, 16)
                        LD5[dmt % 2].n += 16

                def p3_store(sy, dmt):
                    sy.wait_ge(VE5.h, dmt * 2 + 2)
                    sy.dma_start(out=out_d[dmt * 128:(dmt + 1) * 128, :],
                                 in_=outst[dmt % 2][:]).then_inc(STO[dmt % 2].h, 16)
                    STO[dmt % 2].n += 16

                @block.sync
                def _(sy):
                    for hs, vv in p2_final.values():
                        sy.wait_ge(hs, vv)
                    for dmt in range(16):
                        if dmt >= 2:
                            sy.wait_ge(VE5.h, (dmt - 1) * 2)
                        sy.dma_start(out=xt_sb[dmt % 2][:],
                                     in_=xT_c[dmt * 128:(dmt + 1) * 128, :]).then_inc(LX3[dmt % 2].h, 16)
                        LX3[dmt % 2].n += 16
                        if dmt >= 1:
                            p3_store(sy, dmt - 1)
                    p3_store(sy, 15)
                    sy.wait_ge(STO[0].h, 128)
                    sy.wait_ge(STO[1].h, 128)

                @block.tensor
                def _(t):
                    for hs, vv in p2_final.values():
                        t.wait_ge(hs, vv)
                    for dmt in range(16):
                        t.wait_ge(LD5[dmt % 2].h, (dmt // 2 + 1) * 16)
                        if dmt >= 2:
                            t.wait_ge(VE5.h, (dmt - 1) * 2)
                        for dv in range(16):
                            for n2 in range(2):
                                ins = t.matmul(s_ps[dmt % 2][:, n2 * 512:(n2 + 1) * 512],
                                               wo_sb[dmt % 2][:, dv * 128:(dv + 1) * 128],
                                               ctx_sb[:, dv * 1024 + n2 * 512:dv * 1024 + (n2 + 1) * 512],
                                               start=(dv == 0), stop=(dv == 15))
                        ins.then_inc(PE5.h, 1); PE5.n += 1

                @block.vector
                def _(v):
                    for hs, vv in p2_final.values():
                        v.wait_ge(hs, vv)
                    for hs, vv in p1b_final.values():
                        v.wait_ge(hs, vv)
                    for dmt in range(16):
                        v.wait_ge(PE5.h, dmt + 1)
                        v.wait_ge(VE5.h, VE5.n)
                        ins = v.tensor_tensor(prod[:], s_ps[dmt % 2][:],
                                              g_T[:, dmt * 1024:(dmt + 1) * 1024], op=ALU.mult)
                        ins.then_inc(VE5.h, 1); VE5.n += 1
                        v.wait_ge(VE5.h, VE5.n)
                        v.wait_ge(LX3[dmt % 2].h, (dmt // 2 + 1) * 16)
                        if dmt >= 2:
                            v.wait_ge(STO[dmt % 2].h, (dmt // 2) * 16)
                        ins = v.tensor_tensor(outst[dmt % 2][:], prod[:], xt_sb[dmt % 2][:],
                                              op=ALU.add)
                        ins.then_inc(VE5.h, 1); VE5.n += 1



        if dbg:
            DBS = sem("DBS")

            @block.sync
            def _(sy):
                for fin in (p0_final, p1a_final, p1b_final, p2_final):
                    for hs, vv in fin.values():
                        sy.wait_ge(hs, vv)
                nd = 0
                if "p0" in phases:
                    sy.dma_start(out=xn_dbg, in_=xn_T[:]).then_inc(DBS.h, 16); nd += 16
                if "p1b" in phases:
                    sy.dma_start(out=g_dbg, in_=g_T[:]).then_inc(DBS.h, 16); nd += 16
                if "p2" in phases:
                    sy.dma_start(out=ctx_dbg, in_=ctx_sb[:]).then_inc(DBS.h, 16); nd += 16
                if nd:
                    sy.wait_ge(DBS.h, nd)

    return nc


def _rope_tables():
    import os, subprocess, tempfile
    prog = (
        "import numpy as np, jax.numpy as jnp, sys\n"
        "inv = (1.0 / 1024.0) ** jnp.linspace(0.0, 1.0, %d)\n"
        "inv = jnp.concatenate([inv, jnp.zeros(%d)])\n"
        "th = jnp.arange(%d, dtype=jnp.float32)[:, None] * inv[None, :]\n"
        "np.save(sys.argv[1] + 'c.npy', np.asarray(jnp.cos(th), np.float32))\n"
        "np.save(sys.argv[1] + 's.npy', np.asarray(jnp.sin(th), np.float32))\n"
        % (HD // 4, HD // 4, T))
    try:
        with tempfile.NamedTemporaryFile(suffix="_rope", delete=False) as f:
            path = f.name
        env = dict(os.environ)
        env.pop("TRN_TERMINAL_POOL_IPS", None)
        env["JAX_PLATFORMS"] = "cpu"
        npp = env.get("NIX_PYTHONPATH", "")
        if npp:
            env["PYTHONPATH"] = npp + os.pathsep + env.get("PYTHONPATH", "")
        r = subprocess.run([sys.executable, "-c", prog, path], env=env,
                           capture_output=True, timeout=600)
        if r.returncode == 0:
            return (np.load(path + "c.npy"), np.load(path + "s.npy"))
    except Exception:
        pass
    lin = np.linspace(0.0, 1.0, HD // 4).astype(np.float32)
    inv = np.concatenate([np.float32(1.0 / 1024.0) ** lin,
                          np.zeros(HD // 4, np.float32)])
    th = np.arange(T, dtype=np.float32)[:, None] * inv[None, :]
    return np.cos(th.astype(np.float64)).astype(np.float32), \
        np.sin(th.astype(np.float64)).astype(np.float32)


def _host_prep(L0_attn_x, L0_attn_W_qkv, L0_attn_W_out, L0_attn_s,
               L0_ffn_gate_w, L0_ffn_up_w, L0_ffn_down_w,
               L0_ffn_gate_s, L0_ffn_up_s, L0_ffn_down_s):
    bf = ml_dtypes.bfloat16
    x = np.ascontiguousarray(np.asarray(L0_attn_x, np.float32).reshape(T, D))
    Wqkv = np.asarray(L0_attn_W_qkv, np.float32)
    Wout = np.asarray(L0_attn_W_out, np.float32)
    s_in = np.asarray(L0_attn_s, np.float32)
    gate_w = np.asarray(L0_ffn_gate_w, np.float32)
    up_w = np.asarray(L0_ffn_up_w, np.float32)
    down_w = np.asarray(L0_ffn_down_w, np.float32)
    gate_s = np.asarray(L0_ffn_gate_s, np.float32)
    up_s = np.asarray(L0_ffn_up_s, np.float32)
    down_s = np.asarray(L0_ffn_down_s, np.float32)

    shared = {}
    shared["wk_r"] = np.ascontiguousarray(Wqkv[:, 2048:2560].reshape(16, 128, 512))
    shared["wv_r"] = np.ascontiguousarray(Wqkv[:, 2560:3072].reshape(16, 128, 512))
    Wq = Wqkv[:, :2048].reshape(16, 128, NKV, QM, HD)
    wq = np.transpose(Wq, (2, 0, 1, 3, 4))
    wq = np.broadcast_to(wq[:, :, :, :, None, :], (NKV, 16, 128, QM, 2, HD))
    shared["wq_r"] = np.ascontiguousarray(wq).reshape(NKV, 16, 128, 512)
    shared["wout_r"] = np.ascontiguousarray(
        Wout.reshape(16, 128, 16, 128).transpose(2, 1, 0, 3))
    shared["gw_r"] = np.ascontiguousarray(
        gate_w.reshape(NIT, 128, 16, 128).transpose(0, 3, 2, 1))
    shared["uw_r"] = np.ascontiguousarray(
        up_w.reshape(NIT, 128, 16, 128).transpose(0, 3, 2, 1))
    shared["dw_r"] = np.ascontiguousarray(
        down_w.reshape(16, 128, NIT, 128).transpose(0, 3, 2, 1))
    shared["gsc"] = np.ascontiguousarray(gate_s.reshape(NIT, 128).T)
    shared["usc"] = np.ascontiguousarray(up_s.reshape(NIT, 128).T)
    shared["dsc"] = np.ascontiguousarray(down_s.reshape(16, 128).T)
    shared["sink"] = np.ascontiguousarray(s_in.reshape(1, 32))
    PS = np.zeros((128, 128), np.float32)
    for b in range(2):
        B = 64 * b
        for i in range(32):
            PS[B + 32 + i, B + i] = 1.0
            PS[B + i, B + 32 + i] = -1.0
    shared["pswap"] = PS.astype(bf)
    shared["ident"] = np.eye(128, dtype=bf)
    shared["onesr"] = np.ones((1, 128), np.float32)
    shared["konst"] = np.broadcast_to(
        np.array([1e-5, -0.5, 0.125, 0.0], np.float32), (128, 4)).copy()

    cos, sin = _rope_tables()
    COSS = np.ascontiguousarray(np.tile(cos.T, (4, 1)))
    SINS = np.ascontiguousarray(np.tile(sin.T, (4, 1)))

    in_maps = []
    for c in range(NCORES):
        m = dict(shared)
        xc = np.ascontiguousarray(np.roll(x, -TC * c, axis=0))
        m["x_c"] = xc
        m["xT_c"] = np.ascontiguousarray(xc[0:TC].T)
        m["coss"] = np.ascontiguousarray(np.roll(COSS, -TC * c, axis=1))
        m["sins"] = np.ascontiguousarray(np.roll(SINS, -TC * c, axis=1))
        mk = np.ones((2, 4, 128, 1024), np.float32)
        for qc in (0, 1):
            for j, lsp in enumerate(LSPS[qc]):
                for par in (0, 1):
                    ls = 2 * lsp + par
                    kg = (TC * c + 128 * ls + np.arange(128)) % T
                    qg = TC * c + 512 * qc + np.arange(512)
                    dd = qg[None, :] - kg[:, None]
                    bad = (dd >= 0) & (dd <= 511)
                    mk[qc, j, :, par * 512:(par + 1) * 512] = np.where(bad, 0.0, 1.0)
        m["masks"] = np.ascontiguousarray(
            mk.reshape(8, 128, 1024).transpose(1, 0, 2).reshape(128, 8192)).astype(bf)
        in_maps.append(m)
    return in_maps


def kernel(**inputs):
    in_maps = _host_prep(**inputs)
    nc = build_nc()
    res = run_bass_kernel_spmd(nc, in_maps, core_ids=list(range(NCORES)))
    out = np.concatenate(
        [np.asarray(r["out_d"], np.float32).T for r in res.results], axis=0)
    return out.reshape(1, T, D)


# version 04
# speedup vs baseline: 1.1908x; 1.1908x over previous
import sys
for _p in ("/opt/trn_rl_repo",):
    if _p not in sys.path:
        sys.path.append(_p)
import numpy as np
import ml_dtypes
from contextlib import ExitStack

import concourse.bass as bass
import concourse.mybir as mybir
from concourse.bass_utils import run_bass_kernel_spmd

F32 = mybir.dt.float32
BF16 = mybir.dt.bfloat16
AF = mybir.ActivationFunctionType
ALU = mybir.AluOpType

NCORES = 8
T = 8192
TC = T // NCORES
D = 2048
NKV, QM, HD = 8, 4, 64
INTER = 5632
NIT = INTER // 128
NDT = D // 128
NCH = T // 128
LSPS = {0: [30, 31, 0, 1], 1: [0, 1, 2, 3]}

VE_CH = 20
PE_CH = 4
STK_CH = 48


class Sem:
    def __init__(self, ctx, nc, name):
        self.h = ctx.enter_context(nc.semaphore(name))
        self.n = 0


def build_nc(phases=("p0", "p1a", "p1b", "p2", "p3"), dbg=False):
    nc = bass.Bass("TRN2", target_bir_lowering=False)

    def din(name, shape, dt=F32):
        return nc.dram_tensor(name, list(shape), dt, kind="ExternalInput").ap()

    def dscr(name, shape, dt=BF16):
        kind = "ExternalOutput" if dbg else "Internal"
        return nc.dram_tensor(name, list(shape), dt, kind=kind).ap()

    x_c = din("x_c", [T, D])
    xT_c = din("xT_c", [D, TC])
    coss = din("coss", [128, T])
    sins = din("sins", [128, T])
    masks = din("masks", [128, 8 * 1024], BF16)
    wk_r = din("wk_r", [16, 128, 512])
    wv_r = din("wv_r", [16, 128, 512])
    wq_r = din("wq_r", [NKV, 16, 128, 512])
    wout_r = din("wout_r", [16, 128, 16, 128])
    gw_r = din("gw_r", [NIT, 128, 16, 128])
    uw_r = din("uw_r", [NIT, 128, 16, 128])
    dw_r = din("dw_r", [16, 128, NIT, 128])
    gsc = din("gsc", [128, NIT])
    usc = din("usc", [128, NIT])
    dsc = din("dsc", [128, 16])
    sink = din("sink", [1, 32])
    pswap = din("pswap", [128, 128], BF16)
    ident = din("ident", [128, 128], BF16)
    onesr = din("onesr", [1, 128])
    konst = din("konst", [128, 4])

    out_d = nc.dram_tensor("out_d", [D, TC], F32, kind="ExternalOutput").ap()

    kt_d = dscr("kt_d", [NKV, 128, 4096])
    v_d = dscr("v_d", [NKV, 128, 64 * 65])
    q_d = dscr("q_d", [NKV, 128, 4096])
    hdn_d = dscr("hdn_d", [NIT, 128, 1024])
    if dbg:
        xn_dbg = nc.dram_tensor("xn_dbg", [128, 8 * 2048], BF16, kind="ExternalOutput").ap()
        g_dbg = nc.dram_tensor("g_dbg", [128, 16 * 1024], BF16, kind="ExternalOutput").ap()
        ctx_dbg = nc.dram_tensor("ctx_dbg", [128, 16 * 1024], BF16, kind="ExternalOutput").ap()

    ctx = ExitStack()
    with ctx:
        sb = lambda n, s, d: ctx.enter_context(nc.sbuf_tensor(n, list(s), d))
        sem = lambda n: Sem(ctx, nc, n)
        block = ctx.enter_context(nc.Block())

        xn_T = sb("xn_T", [128, 8 * 2048], BF16)
        g_T = sb("g_T", [128, 16 * 1024], BF16)
        ctx_sb = sb("ctx_sb", [128, 16 * 1024], BF16)
        gsc_sb = sb("gsc_sb", [128, NIT], F32)
        usc_sb = sb("usc_sb", [128, NIT], F32)
        dsc_sb = sb("dsc_sb", [128, 16], F32)
        pswap_sb = sb("pswap_sb", [128, 128], BF16)
        ident_sb = sb("ident_sb", [128, 128], BF16)
        ones_sb = sb("ones_sb", [1, 128], F32)
        sink_sb = sb("sink_sb", [1, 32], F32)
        esink_sb = sb("esink_sb", [1, 32], F32)
        konst_sb = sb("konst_sb", [128, 4], F32)

        LDC = sem("LDC")

        @block.gpsimd
        def _(g):
            for dst, src in ((gsc_sb, gsc), (usc_sb, usc), (dsc_sb, dsc),
                             (pswap_sb, pswap), (ident_sb, ident),
                             (ones_sb, onesr), (sink_sb, sink),
                             (konst_sb, konst)):
                g.dma_start(out=dst[:], in_=src).then_inc(LDC.h, 16)
                LDC.n += 16

        xnr = xn_T[:].rearrange("p (ct r) -> p ct r", ct=8)

        def xn_rhs(dt, n2):
            return xnr[:, 4 * n2:4 * n2 + 4, dt * 128:(dt + 1) * 128]

        p0_final = {}
        if "p0" in phases:
            with ExitStack() as p0:
                sb0 = lambda n, s, d: p0.enter_context(nc.sbuf_tensor(n, list(s), d))
                ps0 = lambda n, s, d: p0.enter_context(nc.psum_tensor(n, list(s), d))
                wk_sb = sb0("wk_sb", [128, 16 * 512], BF16)
                wv_sb = sb0("wv_sb", [128, 16 * 512], BF16)
                xch = [sb0(f"xch{i}", [128, 2048], F32) for i in range(2)]
                junk = sb0("junk0", [128, 2048], F32)
                junk2 = sb0("junk2", [128, 2048], F32)
                stat = sb0("stat", [128, 8], F32)
                xnm = sb0("xnm", [128, 2048], BF16)
                xnchT = [sb0(f"xnchT{i}", [128, 2048], BF16) for i in range(2)]
                cosl = sb0("cosl", [128, 2048], F32)
                sinl = sb0("sinl", [128, 2048], F32)
                kbf = [sb0(f"kbf{i}", [128, 512], BF16) for i in range(2)]
                t1s = sb0("t1s", [128, 512], F32)
                t2s = sb0("t2s", [128, 512], F32)
                krope = [sb0(f"krope{i}", [128, 512], BF16) for i in range(2)]
                vbf = [sb0(f"vbf{i}", [128, 8 * 65], BF16) for i in range(2)]
                kps = ps0("kps", [128, 512], F32)
                vps = ps0("vps", [128, 512], F32)
                ksw = ps0("ksw", [128, 512], F32)
                xnT_ps = ps0("xnT_ps", [128, 2048], BF16)

                LDW0, LDR = sem("LDW0"), sem("LDR")
                LDX = [sem("LDX0"), sem("LDX1")]
                VE0, PE0, SE0 = sem("VE0"), sem("PE0"), sem("SE0")
                STK = [sem("STK0"), sem("STK1")]

                @block.gpsimd
                def _(g):
                    for i in range(4):
                        dst = wk_sb[:, i * 2048:(i + 1) * 2048].rearrange(
                            "p (dt c) -> p dt c", dt=4)
                        g.dma_start(out=dst,
                                    in_=wk_r[i * 4:(i + 1) * 4].rearrange(
                                        "dt p c -> p dt c")).then_inc(LDW0.h, 16)
                        LDW0.n += 16
                        dst = wv_sb[:, i * 2048:(i + 1) * 2048].rearrange(
                            "p (dt c) -> p dt c", dt=4)
                        g.dma_start(out=dst,
                                    in_=wv_r[i * 4:(i + 1) * 4].rearrange(
                                        "dt p c -> p dt c")).then_inc(LDW0.h, 16)
                        LDW0.n += 16

                @block.sync
                def _(sy):
                    for ct in range(NCH):
                        if ct % 16 == 0:
                            if ct > 0:
                                sy.wait_ge(VE0.h, (ct - 1) * VE_CH + 19)
                            sy.dma_start(out=cosl[:], in_=coss[:, ct * 128:(ct + 16) * 128]).then_inc(LDR.h, 16)
                            LDR.n += 16
                            sy.dma_start(out=sinl[:], in_=sins[:, ct * 128:(ct + 16) * 128]).then_inc(LDR.h, 16)
                            LDR.n += 16
                        if ct >= 2:
                            sy.wait_ge(VE0.h, (ct - 2) * VE_CH + 7)
                        sy.dma_start(out=xch[ct % 2][:],
                                     in_=x_c[ct * 128:(ct + 1) * 128, :]).then_inc(LDX[ct % 2].h, 16)
                        LDX[ct % 2].n += 16

                @block.scalar
                def _(s):
                    s.wait_ge(LDC.h, LDC.n)
                    for ct in range(NCH):
                        s.wait_ge(VE0.h, ct * 20 + 5)
                        ins = s.activation(stat[:, 7:8], stat[:, 4:5], AF.Ln,
                                           bias=konst_sb[:, 0:1])
                        ins.then_inc(SE0.h, 1); SE0.n += 1
                        s.wait_ge(SE0.h, ct * 2 + 1)
                        ins = s.activation(stat[:, 5:6], stat[:, 7:8], AF.Exp,
                                           scale=konst_sb[:, 1:2])
                        ins.then_inc(SE0.h, 1); SE0.n += 1

                @block.vector
                def _(v):
                    def vdo(ins):
                        ins.then_inc(VE0.h, 1); VE0.n += 1
                        return ins
                    v.memset(vbf[0][:].rearrange("p (h c) -> p h c", h=8)[:, :, 64:65], 1.0)
                    v.memset(vbf[1][:].rearrange("p (h c) -> p h c", h=8)[:, :, 64:65], 1.0)
                    for ct in range(NCH):
                        xc = xch[ct % 2]
                        v.wait_ge(LDX[ct % 2].h, (ct // 2 + 1) * 16)
                        if ct % 16 == 0:
                            v.wait_ge(LDR.h, (ct // 16 + 1) * 32)
                        vdo(v.tensor_scalar(junk[:], xc[:], 1.0, 0.0, op0=ALU.mult,
                                            op1=ALU.add, accum_out=stat[:, 0:1]))
                        vdo(v.scalar_tensor_tensor(junk2[:], xc[:], 1.0, xc[:], op0=ALU.mult,
                                                   op1=ALU.mult, accum_out=stat[:, 1:2]))
                        v.wait_ge(VE0.h, VE0.n)
                        vdo(v.tensor_scalar(stat[:, 2:3], stat[:, 0:1], 1.0 / D, None, op0=ALU.mult))
                        v.wait_ge(VE0.h, VE0.n)
                        vdo(v.tensor_tensor(stat[:, 3:4], stat[:, 2:3], stat[:, 2:3], op=ALU.mult))
                        v.wait_ge(VE0.h, VE0.n)
                        vdo(v.scalar_tensor_tensor(stat[:, 4:5], stat[:, 1:2], 1.0 / D,
                                                   stat[:, 3:4], op0=ALU.mult, op1=ALU.subtract))
                        v.wait_ge(SE0.h, (ct + 1) * 2)
                        vdo(v.tensor_scalar(stat[:, 6:7], stat[:, 2:3], stat[:, 5:6], -1.0,
                                            op0=ALU.mult, op1=ALU.mult))
                        v.wait_ge(VE0.h, VE0.n)
                        vdo(v.tensor_scalar(xnm[:], xc[:], stat[:, 5:6], stat[:, 6:7],
                                            op0=ALU.mult, op1=ALU.add))
                        v.wait_ge(PE0.h, ct * PE_CH + 1)
                        if ct >= 2:
                            v.wait_ge(PE0.h, (ct - 2) * PE_CH + 3)
                        vdo(v.tensor_copy(xnchT[ct % 2][:], xnT_ps[:]))
                        if ct < 8:
                            vdo(v.tensor_copy(xn_T[:, ct * 2048:(ct + 1) * 2048], xnT_ps[:]))
                        else:
                            vdo(v.tensor_copy(junk[:, 0:8], xnT_ps[:, 0:8]))
                        v.wait_ge(PE0.h, ct * PE_CH + 2)
                        if ct >= 2:
                            v.wait_ge(STK[ct % 2].h, (ct // 2) * STK_CH)
                        vdo(v.tensor_copy(kbf[ct % 2][:], kps[:]))
                        v.wait_ge(PE0.h, ct * PE_CH + 4)
                        co = cosl[:, (ct % 16) * 128:(ct % 16 + 1) * 128]
                        si = sinl[:, (ct % 16) * 128:(ct % 16 + 1) * 128]
                        for b in range(4):
                            sl = slice(b * 128, (b + 1) * 128)
                            vdo(v.tensor_tensor(t1s[:, sl], ksw[:, sl], si, op=ALU.mult))
                            vdo(v.tensor_tensor(t2s[:, sl], kps[:, sl], co, op=ALU.mult))
                        v.wait_ge(VE0.h, VE0.n)
                        vdo(v.tensor_tensor(krope[ct % 2][:], t1s[:], t2s[:], op=ALU.add))
                        v.wait_ge(PE0.h, ct * PE_CH + 3)
                        vdo(v.tensor_copy(
                            vbf[ct % 2][:].rearrange("p (h c) -> p h c", h=8)[:, :, 0:64],
                            vps[:].rearrange("p (h c) -> p h c", h=8)))

                @block.tensor
                def _(t):
                    t.wait_ge(LDW0.h, 8 * 16)
                    t.wait_ge(LDC.h, LDC.n)
                    for ct in range(NCH):
                        t.wait_ge(VE0.h, ct * VE_CH + 7)
                        for dt in range(NDT):
                            ins = t.transpose(xnT_ps[:, dt * 128:(dt + 1) * 128],
                                              xnm[:, dt * 128:(dt + 1) * 128], ident_sb[:])
                        ins.then_inc(PE0.h, 1); PE0.n += 1
                        t.wait_ge(VE0.h, ct * VE_CH + 8)
                        if ct >= 1:
                            t.wait_ge(VE0.h, (ct - 1) * VE_CH + 18)
                        xnc = xnchT[ct % 2]
                        for kt4 in range(4):
                            for dt in range(NDT):
                                ins = t.matmul(kps[:, kt4 * 128:(kt4 + 1) * 128],
                                               wk_sb[:, dt * 512 + kt4 * 128:dt * 512 + (kt4 + 1) * 128],
                                               xnc[:, dt * 128:(dt + 1) * 128],
                                               start=(dt == 0), stop=(dt == NDT - 1))
                        ins.then_inc(PE0.h, 1); PE0.n += 1
                        if ct >= 1:
                            t.wait_ge(VE0.h, (ct - 1) * VE_CH + 20)
                        for dt in range(NDT):
                            ins = t.matmul(vps[:],
                                           xnc[:, dt * 128:(dt + 1) * 128],
                                           wv_sb[:, dt * 512:(dt + 1) * 512],
                                           start=(dt == 0), stop=(dt == NDT - 1))
                        ins.then_inc(PE0.h, 1); PE0.n += 1
                        t.wait_ge(VE0.h, ct * VE_CH + 10)
                        ins = t.matmul(ksw[:], pswap_sb[:], kbf[ct % 2][:], start=True, stop=True)
                        ins.then_inc(PE0.h, 1); PE0.n += 1

                kt_dr = kt_d.rearrange("(kt4 hh) p k -> hh p kt4 k", hh=2)
                v_dr = v_d.rearrange("h k (ls c) -> ls k h c", c=65)

                @block.gpsimd
                def _(g):
                    for ct in range(NCH):
                        g.wait_ge(VE0.h, ct * VE_CH + 19)
                        kr = krope[ct % 2]
                        for hh in range(2):
                            dst = kt_dr[hh, (ct % 2) * 64:(ct % 2) * 64 + 64, :,
                                        (ct // 2) * 128:(ct // 2) * 128 + 128]
                            g.dma_start(out=dst,
                                        in_=kr[hh * 64:(hh + 1) * 64, :].rearrange(
                                            "d (kt4 k) -> d kt4 k", kt4=4)).then_inc(STK[ct % 2].h, 16)
                            STK[ct % 2].n += 16
                        g.wait_ge(VE0.h, ct * VE_CH + 20)
                        g.dma_start(out=v_dr[ct],
                                    in_=vbf[ct % 2][:].rearrange("k (h c) -> k h c", h=8)).then_inc(STK[ct % 2].h, 16)
                        STK[ct % 2].n += 16

                p0_final = dict(VE0=(VE0.h, VE0.n), PE0=(PE0.h, PE0.n),
                                SE0=(SE0.h, SE0.n), STK0=(STK[0].h, STK[0].n),
                                STK1=(STK[1].h, STK[1].n))

        p1a_final = {}
        if "p1a" in phases:
            with ExitStack() as p1:
                sb1 = lambda n, s, d: p1.enter_context(nc.sbuf_tensor(n, list(s), d))
                ps1 = lambda n, s, d: p1.enter_context(nc.psum_tensor(n, list(s), d))
                gw_sb = [sb1(f"gw{i}", [128, 2048], BF16) for i in range(2)]
                uw_sb = [sb1(f"uw{i}", [128, 2048], BF16) for i in range(2)]
                sil = [sb1(f"sil{i}", [128, 1024], F32) for i in range(2)]
                hdnst = [sb1(f"hdnst{i}", [128, 1024], BF16) for i in range(2)]
                g_ps = [ps1(f"g_ps{i}", [128, 1024], F32) for i in range(2)]
                u_ps = [ps1(f"u_ps{i}", [128, 1024], F32) for i in range(2)]

                LD1 = [sem("LD1a"), sem("LD1b")]
                STH = [sem("STHa"), sem("STHb")]
                PE1, VE1, SE1 = sem("PE1"), sem("VE1"), sem("SE1")

                def p1a_store(g, it):
                    g.wait_ge(VE1.h, it + 1)
                    g.dma_start(out=hdn_d[it], in_=hdnst[it % 2][:]).then_inc(STH[it % 2].h, 16)
                    STH[it % 2].n += 16

                @block.gpsimd
                def _(g):
                    for hs, vv in p0_final.values():
                        g.wait_ge(hs, vv)
                    for it in range(NIT):
                        if it >= 2:
                            g.wait_ge(PE1.h, (it - 1) * 2)
                        g.dma_start(out=gw_sb[it % 2][:].rearrange("p (dt c) -> p dt c", dt=16),
                                    in_=gw_r[it]).then_inc(LD1[it % 2].h, 16)
                        LD1[it % 2].n += 16
                        g.dma_start(out=uw_sb[it % 2][:].rearrange("p (dt c) -> p dt c", dt=16),
                                    in_=uw_r[it]).then_inc(LD1[it % 2].h, 16)
                        LD1[it % 2].n += 16
                        if it >= 2:
                            p1a_store(g, it - 2)
                    p1a_store(g, NIT - 2)
                    p1a_store(g, NIT - 1)

                @block.tensor
                def _(t):
                    for hs, vv in p0_final.values():
                        t.wait_ge(hs, vv)
                    for it in range(NIT):
                        t.wait_ge(LD1[it % 2].h, (it // 2 + 1) * 32)
                        if it >= 2:
                            t.wait_ge(SE1.h, it - 1)
                        for dt in range(NDT):
                            for n2 in range(2):
                                ins = t.matmul(g_ps[it % 2][:, n2 * 512:(n2 + 1) * 512],
                                               gw_sb[it % 2][:, dt * 128:(dt + 1) * 128],
                                               xn_rhs(dt, n2),
                                               start=(dt == 0), stop=(dt == NDT - 1))
                        ins.then_inc(PE1.h, 1); PE1.n += 1
                        if it >= 2:
                            t.wait_ge(VE1.h, it - 1)
                        for dt in range(NDT):
                            for n2 in range(2):
                                ins = t.matmul(u_ps[it % 2][:, n2 * 512:(n2 + 1) * 512],
                                               uw_sb[it % 2][:, dt * 128:(dt + 1) * 128],
                                               xn_rhs(dt, n2),
                                               start=(dt == 0), stop=(dt == NDT - 1))
                        ins.then_inc(PE1.h, 1); PE1.n += 1

                @block.scalar
                def _(s):
                    s.wait_ge(LDC.h, LDC.n)
                    for it in range(NIT):
                        s.wait_ge(PE1.h, it * 2 + 1)
                        if it >= 2:
                            s.wait_ge(VE1.h, it - 1)
                        ins = s.activation(sil[it % 2][:], g_ps[it % 2][:], AF.Silu,
                                           scale=gsc_sb[:, it:it + 1])
                        ins.then_inc(SE1.h, 1); SE1.n += 1

                @block.vector
                def _(v):
                    for hs, vv in p0_final.values():
                        v.wait_ge(hs, vv)
                    for it in range(NIT):
                        v.wait_ge(PE1.h, it * 2 + 2)
                        v.wait_ge(SE1.h, it + 1)
                        if it >= 2:
                            v.wait_ge(STH[it % 2].h, (it // 2) * 16)
                        ins = v.scalar_tensor_tensor(hdnst[it % 2][:], u_ps[it % 2][:],
                                                     usc_sb[:, it:it + 1], sil[it % 2][:],
                                                     op0=ALU.mult, op1=ALU.mult)
                        ins.then_inc(VE1.h, 1); VE1.n += 1

                p1a_final = dict(PE1=(PE1.h, PE1.n), VE1=(VE1.h, VE1.n),
                                 SE1=(SE1.h, SE1.n), STHa=(STH[0].h, STH[0].n),
                                 STHb=(STH[1].h, STH[1].n))

        p1b_final = {}
        if "p1b" in phases:
            with ExitStack() as p1b:
                sb2 = lambda n, s, d: p1b.enter_context(nc.sbuf_tensor(n, list(s), d))
                ps2 = lambda n, s, d: p1b.enter_context(nc.psum_tensor(n, list(s), d))
                dw_sb = [sb2(f"dw{i}", [128, NIT * 128], BF16) for i in range(2)]
                hdn_half = sb2("hdn_half", [128, NIT * 512], BF16)
                d_ps = [ps2(f"d_ps{i}", [128, 512], F32) for i in range(2)]

                LD2 = [sem("LD2a"), sem("LD2b")]
                LDH = sem("LDH")
                PE2, SE2 = sem("PE2"), sem("SE2")
                hdn_r = hdn_d.rearrange("it p t -> p it t")

                @block.sync
                def _(sy):
                    for hs, vv in p1a_final.values():
                        sy.wait_ge(hs, vv)
                    for th in range(2):
                        if th == 1:
                            sy.wait_ge(PE2.h, 16)
                        src = hdn_r[:, :, th * 512:(th + 1) * 512]
                        for q in range(4):
                            sy.dma_start(out=hdn_half[:, q * 11 * 512:(q + 1) * 11 * 512]
                                         .rearrange("p (it t) -> p it t", it=11),
                                         in_=src[:, q * 11:(q + 1) * 11, :]).then_inc(LDH.h, 16)
                            LDH.n += 16

                @block.gpsimd
                def _(g):
                    for hs, vv in p1a_final.values():
                        g.wait_ge(hs, vv)
                    for it2 in range(32):
                        dmt = it2 % 16
                        if it2 >= 2:
                            g.wait_ge(PE2.h, it2 - 1)
                        g.dma_start(out=dw_sb[it2 % 2][:].rearrange("p (it c) -> p it c", it=NIT),
                                    in_=dw_r[dmt]).then_inc(LD2[it2 % 2].h, 16)
                        LD2[it2 % 2].n += 16

                @block.tensor
                def _(t):
                    for hs, vv in p1a_final.values():
                        t.wait_ge(hs, vv)
                    for it2 in range(32):
                        th, dmt = it2 // 16, it2 % 16
                        t.wait_ge(LD2[it2 % 2].h, (it2 // 2 + 1) * 16)
                        t.wait_ge(LDH.h, (th + 1) * 64)
                        if it2 >= 2:
                            t.wait_ge(SE2.h, it2 - 1)
                        for it in range(NIT):
                            ins = t.matmul(d_ps[it2 % 2][:],
                                           dw_sb[it2 % 2][:, it * 128:(it + 1) * 128],
                                           hdn_half[:, it * 512:(it + 1) * 512],
                                           start=(it == 0), stop=(it == NIT - 1))
                        ins.then_inc(PE2.h, 1); PE2.n += 1

                @block.scalar
                def _(s):
                    s.wait_ge(LDC.h, LDC.n)
                    for it2 in range(32):
                        th, dmt = it2 // 16, it2 % 16
                        s.wait_ge(PE2.h, it2 + 1)
                        ins = s.activation(g_T[:, dmt * 1024 + th * 512:dmt * 1024 + th * 512 + 512],
                                           d_ps[it2 % 2][:], AF.Silu, scale=dsc_sb[:, dmt:dmt + 1])
                        ins.then_inc(SE2.h, 1); SE2.n += 1

                p1b_final = dict(PE2=(PE2.h, PE2.n), SE2=(SE2.h, SE2.n))

        p2_final = {}
        if "p2" in phases:
            with ExitStack() as p2o:
                sb3 = lambda n, s, d: p2o.enter_context(nc.sbuf_tensor(n, list(s), d))
                masks_sb = sb3("masks_sb", [128, 8 * 1024], BF16)
                LDM = sem("LDM")
                SE3 = sem("SE3")

                @block.sync
                def _(sy):
                    for hs, vv in p1b_final.values():
                        sy.wait_ge(hs, vv)
                    sy.dma_start(out=masks_sb[:], in_=masks).then_inc(LDM.h, 16)
                    LDM.n += 16

                @block.scalar
                def _(s):
                    s.wait_ge(LDC.h, LDC.n)
                    ins = s.activation(esink_sb[:], sink_sb[:], AF.Exp)
                    ins.then_inc(SE3.h, 1); SE3.n += 1

                LQR, LDQW, STQ = sem("LQR"), sem("LDQW"), sem("STQ")
                PE3, VE3 = sem("PE3"), sem("VE3")

                with ExitStack() as p2p:
                    sbp = lambda n, s, d: p2p.enter_context(nc.sbuf_tensor(n, list(s), d))
                    psp = lambda n, s, d: p2p.enter_context(nc.psum_tensor(n, list(s), d))
                    wqh_sb = sbp("wqh_sb", [128, 16 * 512], BF16)
                    qbf = sbp("qbf", [128, 1024], BF16)
                    qstage = sbp("qstage", [128, 4096], BF16)
                    cosq = sbp("cosq", [128, 1024], F32)
                    sinq = sbp("sinq", [128, 1024], F32)
                    t1q = sbp("t1q", [128, 1024], F32)
                    t2q = sbp("t2q", [128, 1024], F32)
                    qp = psp("qp", [128, 1024], F32)
                    qsw = psp("qsw", [128, 1024], F32)

                    @block.sync
                    def _(sy):
                        sy.dma_start(out=cosq[:], in_=coss[:, 0:1024]).then_inc(LQR.h, 16)
                        sy.dma_start(out=sinq[:], in_=sins[:, 0:1024]).then_inc(LQR.h, 16)
                        LQR.n += 32

                    def p2p_store(g, h):
                        g.wait_ge(VE3.h, (h + 1) * 16)
                        g.dma_start(out=q_d[h], in_=qstage[:]).then_inc(STQ.h, 16)
                        STQ.n += 16

                    @block.gpsimd
                    def _(g):
                        for hs, vv in p1b_final.values():
                            g.wait_ge(hs, vv)
                        for h in range(NKV):
                            if h >= 1:
                                g.wait_ge(PE3.h, 8 * h)
                            g.dma_start(out=wqh_sb[:].rearrange("p (dt c) -> p dt c", dt=16),
                                        in_=wq_r[h].rearrange("dt p c -> p dt c")).then_inc(LDQW.h, 16)
                            LDQW.n += 16
                            if h >= 1:
                                p2p_store(g, h - 1)
                        p2p_store(g, NKV - 1)

                    @block.tensor
                    def _(t):
                        for hs, vv in p1b_final.values():
                            t.wait_ge(hs, vv)
                        for h in range(NKV):
                            t.wait_ge(LDQW.h, (h + 1) * 16)
                            for m in range(QM):
                                idx = h * 4 + m
                                if idx >= 1:
                                    t.wait_ge(VE3.h, idx * 4 - 1)
                                for dt in range(NDT):
                                    for n2 in range(2):
                                        ins = t.matmul(qp[:, n2 * 512:(n2 + 1) * 512],
                                                       wqh_sb[:, dt * 512 + m * 128:dt * 512 + (m + 1) * 128],
                                                       xn_rhs(dt, n2),
                                                       start=(dt == 0), stop=(dt == NDT - 1))
                                ins.then_inc(PE3.h, 1); PE3.n += 1
                                t.wait_ge(VE3.h, idx * 4 + 1)
                                t.matmul(qsw[:, 0:512], pswap_sb[:], qbf[:, 0:512],
                                         start=True, stop=True)
                                ins = t.matmul(qsw[:, 512:1024], pswap_sb[:], qbf[:, 512:1024],
                                               start=True, stop=True)
                                ins.then_inc(PE3.h, 1); PE3.n += 1

                    @block.vector
                    def _(v):
                        def vdo(ins):
                            ins.then_inc(VE3.h, 1); VE3.n += 1
                            return ins
                        v.wait_ge(LQR.h, 32)
                        for hs, vv in p1b_final.values():
                            v.wait_ge(hs, vv)
                        for h in range(NKV):
                            for m in range(QM):
                                idx = h * 4 + m
                                v.wait_ge(PE3.h, idx * 2 + 1)
                                if m == 0 and h >= 1:
                                    v.wait_ge(STQ.h, 16 * h)
                                vdo(v.tensor_copy(qbf[:], qp[:]))
                                v.wait_ge(PE3.h, idx * 2 + 2)
                                vdo(v.tensor_tensor(t1q[:], qsw[:], sinq[:], op=ALU.mult))
                                vdo(v.tensor_tensor(t2q[:], qp[:], cosq[:], op=ALU.mult))
                                v.wait_ge(VE3.h, VE3.n)
                                vdo(v.tensor_tensor(qstage[:, m * 1024:(m + 1) * 1024],
                                                    t1q[:], t2q[:], op=ALU.add))

                    pe3_fin, ve3_fin, stq_fin = PE3.n, VE3.n, STQ.n

                with ExitStack() as p2m:
                    sbm = lambda n, s, d: p2m.enter_context(nc.sbuf_tensor(n, list(s), d))
                    psm = lambda n, s, d: p2m.enter_context(nc.psum_tensor(n, list(s), d))
                    KT = [sbm(f"KT{i}", [128, 4096], BF16) for i in range(2)]
                    VS = [sbm(f"VS{i}", [128, 64 * 65], BF16) for i in range(2)]
                    QS = [sbm(f"QS{i}", [128, 4096], BF16) for i in range(2)]
                    ES = [sbm(f"ES{i}", [128, 1024], BF16) for i in range(8)]
                    rbc = sbm("rbc", [128, 2048], F32)
                    dsum = sbm("dsum", [1, 2048], F32)
                    scores = [psm(f"scores{i}", [128, 1024], F32) for i in range(2)]
                    ctxp = psm("ctxp", [65, 2048], F32)

                    LKV = [sem("LKVa"), sem("LKVb")]
                    QKs, EXs, WVs, MKs, BCs, VE4 = (sem("QKs"), sem("EXs"), sem("WVs"),
                                                    sem("MKs"), sem("BCs"), sem("VE4"))

                    def masked_slot(qc, lsp):
                        return LSPS[qc].index(lsp) if lsp in LSPS[qc] else None

                    mk_after = {}
                    _cnt = 0
                    for _h in range(NKV):
                        for _qc in range(2):
                            for _lsp in range(32):
                                for _m in range(QM):
                                    _n = ((_h * 2 + _qc) * 32 + _lsp) * 4 + _m
                                    if masked_slot(_qc, _lsp) is not None:
                                        _cnt += 1
                                    mk_after[_n] = _cnt

                    @block.sync
                    def _(sy):
                        sy.wait_ge(LDM.h, 16)
                        sy.wait_ge(PE3.h, pe3_fin)
                        sy.wait_ge(VE3.h, ve3_fin)
                        sy.wait_ge(STQ.h, stq_fin)
                        if "p0" in phases:
                            for kk in ("STK0", "STK1"):
                                hs, vv = p0_final[kk]
                                sy.wait_ge(hs, vv)
                        for h in range(NKV):
                            if h >= 2:
                                sy.wait_ge(QKs.h, (h - 1) * 256)
                                sy.wait_ge(WVs.h, (h - 1) * 64)
                            sy.dma_start(out=KT[h % 2][:], in_=kt_d[h]).then_inc(LKV[h % 2].h, 16)
                            sy.dma_start(out=VS[h % 2][:], in_=v_d[h]).then_inc(LKV[h % 2].h, 16)
                            sy.wait_ge(STQ.h, 16 * (h + 1))
                            sy.dma_start(out=QS[h % 2][:], in_=q_d[h]).then_inc(LKV[h % 2].h, 16)
                            LKV[h % 2].n += 48

                    def emit_wv_batch(t, h, L, lsp):
                        g_last = L * 128 + lsp * 4 + 3
                        if masked_slot(L % 2, lsp) is not None:
                            t.wait_ge(MKs.h, mk_after[g_last])
                        else:
                            t.wait_ge(EXs.h, g_last + 1)
                        if lsp == 0:
                            t.wait_ge(VE4.h, L * 11)
                        for par in range(2):
                            for m in range(QM):
                                g = L * 128 + lsp * 4 + m
                                ins = t.matmul(ctxp[:, m * 512:(m + 1) * 512],
                                               VS[h % 2][:, (2 * lsp + par) * 65:(2 * lsp + par) * 65 + 65],
                                               ES[g % 8][:, par * 512:(par + 1) * 512],
                                               start=(lsp == 0 and par == 0),
                                               stop=(lsp == 31 and par == 1))
                        ins.then_inc(WVs.h, 1); WVs.n += 1

                    @block.tensor
                    def _(t):
                        t.wait_ge(PE3.h, pe3_fin)
                        for h in range(NKV):
                            t.wait_ge(LKV[h % 2].h, (h // 2 + 1) * 48)
                            for qc in range(2):
                                L = h * 2 + qc
                                for lsp in range(32):
                                    for m in range(QM):
                                        n = L * 128 + lsp * 4 + m
                                        if n >= 2:
                                            t.wait_ge(EXs.h, n - 1)
                                        if lsp == 0 and m == 0 and L >= 1:
                                            t.wait_ge(VE4.h, (L - 1) * 11 + 7)
                                        qs = QS[h % 2]
                                        csl = slice(m * 1024 + qc * 512, m * 1024 + (qc + 1) * 512)
                                        t.matmul(scores[n % 2][:, 0:512],
                                                 KT[h % 2][0:64, lsp * 128:(lsp + 1) * 128],
                                                 qs[0:64, csl], start=True, stop=True,
                                                 tile_position=(0, 0))
                                        ins = t.matmul(scores[n % 2][:, 512:1024],
                                                       KT[h % 2][64:128, lsp * 128:(lsp + 1) * 128],
                                                       qs[64:128, csl], start=True, stop=True,
                                                       tile_position=(64, 0))
                                        ins.then_inc(QKs.h, 1); QKs.n += 1
                                    if lsp >= 1:
                                        emit_wv_batch(t, h, L, lsp - 1)
                                emit_wv_batch(t, h, L, 31)
                                t.wait_ge(VE4.h, L * 11 + 5)
                                t.wait_ge(EXs.h, (L + 1) * 128)
                                for half in range(2):
                                    for q2 in range(2):
                                        ins = t.matmul(scores[half][:, q2 * 512:(q2 + 1) * 512],
                                                       ones_sb[:],
                                                       dsum[0:1, half * 1024 + q2 * 512:half * 1024 + (q2 + 1) * 512],
                                                       start=True, stop=True)
                                    ins.then_inc(BCs.h, 1); BCs.n += 1

                    @block.scalar
                    def _(s):
                        s.wait_ge(LDC.h, LDC.n)
                        for n in range(NKV * 2 * 128):
                            s.wait_ge(QKs.h, n + 1)
                            B = n // 4
                            if B >= 2:
                                s.wait_ge(WVs.h, B - 1)
                            ins = s.activation(ES[n % 8][:], scores[n % 2][:],
                                               AF.Exp, scale=konst_sb[:, 2:3])
                            ins.then_inc(EXs.h, 1); EXs.n += 1

                    @block.vector
                    def _(v):
                        v.wait_ge(SE3.h, 1)
                        v.wait_ge(LDM.h, 16)
                        for h in range(NKV):
                            for qc in range(2):
                                L = h * 2 + qc
                                for lsp in range(32):
                                    slot = masked_slot(qc, lsp)
                                    if slot is None:
                                        continue
                                    for m in range(QM):
                                        n = L * 128 + lsp * 4 + m
                                        v.wait_ge(EXs.h, n + 1)
                                        sidx = qc * 4 + slot
                                        ins = v.tensor_tensor(ES[n % 8][:], ES[n % 8][:],
                                                              masks_sb[:, sidx * 1024:(sidx + 1) * 1024],
                                                              op=ALU.mult)
                                        ins.then_inc(MKs.h, 1); MKs.n += 1
                                def vdo4(ins):
                                    ins.then_inc(VE4.h, 1); VE4.n += 1
                                    return ins
                                v.wait_ge(WVs.h, (L + 1) * 32)
                                vdo4(v.tensor_copy(dsum[:], ctxp[64:65, :]))
                                v.wait_ge(VE4.h, VE4.n)
                                for m in range(QM):
                                    vdo4(v.tensor_scalar(dsum[0:1, m * 512:(m + 1) * 512],
                                                         dsum[0:1, m * 512:(m + 1) * 512],
                                                         esink_sb[0:1, h * 4 + m:h * 4 + m + 1],
                                                         None, op0=ALU.add))
                                v.wait_ge(BCs.h, L * 2 + 1)
                                vdo4(v.reciprocal(rbc[:, 0:1024], scores[0][:]))
                                v.wait_ge(BCs.h, L * 2 + 2)
                                vdo4(v.reciprocal(rbc[:, 1024:2048], scores[1][:]))
                                v.wait_ge(VE4.h, VE4.n)
                                for m in range(QM):
                                    dvt = (h * 4 + m) // 2
                                    r0 = (m % 2) * 64
                                    vdo4(v.tensor_tensor(
                                        ctx_sb[r0:r0 + 64, dvt * 1024 + qc * 512:dvt * 1024 + qc * 512 + 512],
                                        ctxp[0:64, m * 512:(m + 1) * 512],
                                        rbc[0:64, m * 512:(m + 1) * 512], op=ALU.mult))

                    p2_final = dict(QKs=(QKs.h, QKs.n), EXs=(EXs.h, EXs.n),
                                    WVs=(WVs.h, WVs.n), VE4=(VE4.h, VE4.n),
                                    BCs=(BCs.h, BCs.n), MKs=(MKs.h, MKs.n))

        if "p3" in phases:
            with ExitStack() as p3:
                sb4 = lambda n, s, d: p3.enter_context(nc.sbuf_tensor(n, list(s), d))
                ps4 = lambda n, s, d: p3.enter_context(nc.psum_tensor(n, list(s), d))
                wo_sb = [sb4(f"wo{i}", [128, 2048], BF16) for i in range(2)]
                xt_sb = [sb4(f"xt{i}", [128, 1024], F32) for i in range(2)]
                prod = sb4("prod", [128, 1024], F32)
                outst = [sb4(f"outst{i}", [128, 1024], F32) for i in range(2)]
                s_ps = [ps4(f"s_ps{i}", [128, 1024], F32) for i in range(2)]

                LD5 = [sem("LD5a"), sem("LD5b")]
                LX3 = [sem("LX3a"), sem("LX3b")]
                STO = [sem("STOa"), sem("STOb")]
                PE5, VE5 = sem("PE5"), sem("VE5")

                @block.gpsimd
                def _(g):
                    for hs, vv in p2_final.values():
                        g.wait_ge(hs, vv)
                    for dmt in range(16):
                        if dmt >= 2:
                            g.wait_ge(PE5.h, dmt - 1)
                        g.dma_start(out=wo_sb[dmt % 2][:].rearrange("p (dv c) -> p dv c", dv=16),
                                    in_=wout_r[dmt]).then_inc(LD5[dmt % 2].h, 16)
                        LD5[dmt % 2].n += 16

                def p3_store(sy, dmt):
                    sy.wait_ge(VE5.h, dmt * 2 + 2)
                    sy.dma_start(out=out_d[dmt * 128:(dmt + 1) * 128, :],
                                 in_=outst[dmt % 2][:]).then_inc(STO[dmt % 2].h, 16)
                    STO[dmt % 2].n += 16

                @block.sync
                def _(sy):
                    for hs, vv in p2_final.values():
                        sy.wait_ge(hs, vv)
                    for dmt in range(16):
                        if dmt >= 2:
                            sy.wait_ge(VE5.h, (dmt - 1) * 2)
                        sy.dma_start(out=xt_sb[dmt % 2][:],
                                     in_=xT_c[dmt * 128:(dmt + 1) * 128, :]).then_inc(LX3[dmt % 2].h, 16)
                        LX3[dmt % 2].n += 16
                        if dmt >= 1:
                            p3_store(sy, dmt - 1)
                    p3_store(sy, 15)
                    sy.wait_ge(STO[0].h, 128)
                    sy.wait_ge(STO[1].h, 128)

                @block.tensor
                def _(t):
                    for hs, vv in p2_final.values():
                        t.wait_ge(hs, vv)
                    for dmt in range(16):
                        t.wait_ge(LD5[dmt % 2].h, (dmt // 2 + 1) * 16)
                        if dmt >= 2:
                            t.wait_ge(VE5.h, (dmt - 1) * 2)
                        for dv in range(16):
                            for n2 in range(2):
                                ins = t.matmul(s_ps[dmt % 2][:, n2 * 512:(n2 + 1) * 512],
                                               wo_sb[dmt % 2][:, dv * 128:(dv + 1) * 128],
                                               ctx_sb[:, dv * 1024 + n2 * 512:dv * 1024 + (n2 + 1) * 512],
                                               start=(dv == 0), stop=(dv == 15))
                        ins.then_inc(PE5.h, 1); PE5.n += 1

                @block.vector
                def _(v):
                    for hs, vv in p2_final.values():
                        v.wait_ge(hs, vv)
                    for hs, vv in p1b_final.values():
                        v.wait_ge(hs, vv)
                    for dmt in range(16):
                        v.wait_ge(PE5.h, dmt + 1)
                        v.wait_ge(VE5.h, VE5.n)
                        ins = v.tensor_tensor(prod[:], s_ps[dmt % 2][:],
                                              g_T[:, dmt * 1024:(dmt + 1) * 1024], op=ALU.mult)
                        ins.then_inc(VE5.h, 1); VE5.n += 1
                        v.wait_ge(VE5.h, VE5.n)
                        v.wait_ge(LX3[dmt % 2].h, (dmt // 2 + 1) * 16)
                        if dmt >= 2:
                            v.wait_ge(STO[dmt % 2].h, (dmt // 2) * 16)
                        ins = v.tensor_tensor(outst[dmt % 2][:], prod[:], xt_sb[dmt % 2][:],
                                              op=ALU.add)
                        ins.then_inc(VE5.h, 1); VE5.n += 1



        if dbg:
            DBS = sem("DBS")

            @block.sync
            def _(sy):
                for fin in (p0_final, p1a_final, p1b_final, p2_final):
                    for hs, vv in fin.values():
                        sy.wait_ge(hs, vv)
                nd = 0
                if "p0" in phases:
                    sy.dma_start(out=xn_dbg, in_=xn_T[:]).then_inc(DBS.h, 16); nd += 16
                if "p1b" in phases:
                    sy.dma_start(out=g_dbg, in_=g_T[:]).then_inc(DBS.h, 16); nd += 16
                if "p2" in phases:
                    sy.dma_start(out=ctx_dbg, in_=ctx_sb[:]).then_inc(DBS.h, 16); nd += 16
                if nd:
                    sy.wait_ge(DBS.h, nd)

    return nc


def _rope_tables():
    import os, subprocess, tempfile
    prog = (
        "import numpy as np, jax.numpy as jnp, sys\n"
        "inv = (1.0 / 1024.0) ** jnp.linspace(0.0, 1.0, %d)\n"
        "inv = jnp.concatenate([inv, jnp.zeros(%d)])\n"
        "th = jnp.arange(%d, dtype=jnp.float32)[:, None] * inv[None, :]\n"
        "np.save(sys.argv[1] + 'c.npy', np.asarray(jnp.cos(th), np.float32))\n"
        "np.save(sys.argv[1] + 's.npy', np.asarray(jnp.sin(th), np.float32))\n"
        % (HD // 4, HD // 4, T))
    try:
        with tempfile.NamedTemporaryFile(suffix="_rope", delete=False) as f:
            path = f.name
        env = dict(os.environ)
        env.pop("TRN_TERMINAL_POOL_IPS", None)
        env["JAX_PLATFORMS"] = "cpu"
        npp = env.get("NIX_PYTHONPATH", "")
        if npp:
            env["PYTHONPATH"] = npp + os.pathsep + env.get("PYTHONPATH", "")
        r = subprocess.run([sys.executable, "-c", prog, path], env=env,
                           capture_output=True, timeout=600)
        if r.returncode == 0:
            return (np.load(path + "c.npy"), np.load(path + "s.npy"))
    except Exception:
        pass
    lin = np.linspace(0.0, 1.0, HD // 4).astype(np.float32)
    inv = np.concatenate([np.float32(1.0 / 1024.0) ** lin,
                          np.zeros(HD // 4, np.float32)])
    th = np.arange(T, dtype=np.float32)[:, None] * inv[None, :]
    return np.cos(th.astype(np.float64)).astype(np.float32), \
        np.sin(th.astype(np.float64)).astype(np.float32)


def _host_prep(L0_attn_x, L0_attn_W_qkv, L0_attn_W_out, L0_attn_s,
               L0_ffn_gate_w, L0_ffn_up_w, L0_ffn_down_w,
               L0_ffn_gate_s, L0_ffn_up_s, L0_ffn_down_s):
    bf = ml_dtypes.bfloat16
    x = np.ascontiguousarray(np.asarray(L0_attn_x, np.float32).reshape(T, D))
    Wqkv = np.asarray(L0_attn_W_qkv, np.float32)
    Wout = np.asarray(L0_attn_W_out, np.float32)
    s_in = np.asarray(L0_attn_s, np.float32)
    gate_w = np.asarray(L0_ffn_gate_w, np.float32)
    up_w = np.asarray(L0_ffn_up_w, np.float32)
    down_w = np.asarray(L0_ffn_down_w, np.float32)
    gate_s = np.asarray(L0_ffn_gate_s, np.float32)
    up_s = np.asarray(L0_ffn_up_s, np.float32)
    down_s = np.asarray(L0_ffn_down_s, np.float32)

    shared = {}
    shared["wk_r"] = np.ascontiguousarray(Wqkv[:, 2048:2560].reshape(16, 128, 512))
    shared["wv_r"] = np.ascontiguousarray(Wqkv[:, 2560:3072].reshape(16, 128, 512))
    Wq = Wqkv[:, :2048].reshape(16, 128, NKV, QM, HD)
    wq = np.transpose(Wq, (2, 0, 1, 3, 4))
    wq = np.broadcast_to(wq[:, :, :, :, None, :], (NKV, 16, 128, QM, 2, HD))
    shared["wq_r"] = np.ascontiguousarray(wq).reshape(NKV, 16, 128, 512)
    shared["wout_r"] = np.ascontiguousarray(
        Wout.reshape(16, 128, 16, 128).transpose(2, 1, 0, 3))
    shared["gw_r"] = np.ascontiguousarray(
        gate_w.reshape(NIT, 128, 16, 128).transpose(0, 3, 2, 1))
    shared["uw_r"] = np.ascontiguousarray(
        up_w.reshape(NIT, 128, 16, 128).transpose(0, 3, 2, 1))
    shared["dw_r"] = np.ascontiguousarray(
        down_w.reshape(16, 128, NIT, 128).transpose(0, 3, 2, 1))
    shared["gsc"] = np.ascontiguousarray(gate_s.reshape(NIT, 128).T)
    shared["usc"] = np.ascontiguousarray(up_s.reshape(NIT, 128).T)
    shared["dsc"] = np.ascontiguousarray(down_s.reshape(16, 128).T)
    shared["sink"] = np.ascontiguousarray(s_in.reshape(1, 32))
    PS = np.zeros((128, 128), np.float32)
    for b in range(2):
        B = 64 * b
        for i in range(32):
            PS[B + 32 + i, B + i] = 1.0
            PS[B + i, B + 32 + i] = -1.0
    shared["pswap"] = PS.astype(bf)
    shared["ident"] = np.eye(128, dtype=bf)
    shared["onesr"] = np.ones((1, 128), np.float32)
    shared["konst"] = np.broadcast_to(
        np.array([1e-5, -0.5, 0.125, 0.0], np.float32), (128, 4)).copy()

    cos, sin = _rope_tables()
    COSS = np.ascontiguousarray(np.tile(cos.T, (4, 1)))
    SINS = np.ascontiguousarray(np.tile(sin.T, (4, 1)))

    in_maps = []
    for c in range(NCORES):
        m = dict(shared)
        xc = np.ascontiguousarray(np.roll(x, -TC * c, axis=0))
        m["x_c"] = xc
        m["xT_c"] = np.ascontiguousarray(xc[0:TC].T)
        m["coss"] = np.ascontiguousarray(np.roll(COSS, -TC * c, axis=1))
        m["sins"] = np.ascontiguousarray(np.roll(SINS, -TC * c, axis=1))
        mk = np.ones((2, 4, 128, 1024), np.float32)
        for qc in (0, 1):
            for j, lsp in enumerate(LSPS[qc]):
                for par in (0, 1):
                    ls = 2 * lsp + par
                    kg = (TC * c + 128 * ls + np.arange(128)) % T
                    qg = TC * c + 512 * qc + np.arange(512)
                    dd = qg[None, :] - kg[:, None]
                    bad = (dd >= 0) & (dd <= 511)
                    mk[qc, j, :, par * 512:(par + 1) * 512] = np.where(bad, 0.0, 1.0)
        m["masks"] = np.ascontiguousarray(
            mk.reshape(8, 128, 1024).transpose(1, 0, 2).reshape(128, 8192)).astype(bf)
        in_maps.append(m)
    return in_maps


def kernel(**inputs):
    in_maps = _host_prep(**inputs)
    nc = build_nc()
    res = run_bass_kernel_spmd(nc, in_maps, core_ids=list(range(NCORES)))
    out = np.concatenate(
        [np.asarray(r["out_d"], np.float32).T for r in res.results], axis=0)
    return out.reshape(1, T, D)


# version 06
# speedup vs baseline: 1.7941x; 1.5066x over previous
import sys
for _p in ("/opt/trn_rl_repo",):
    if _p not in sys.path:
        sys.path.append(_p)
import numpy as np
import ml_dtypes
from contextlib import ExitStack

import concourse.bass as bass
import concourse.mybir as mybir
from concourse.bass_utils import run_bass_kernel_spmd
from concourse.aot_env import allow_ici_all_to_all

F32 = mybir.dt.float32
BF16 = mybir.dt.bfloat16
AF = mybir.ActivationFunctionType
ALU = mybir.AluOpType

NCORES = 8
T = 8192
TC = T // NCORES
D = 2048
NKV, QM, HD = 8, 4, 64
INTER = 5632
NIT = INTER // 128
NDT = D // 128
NOC = TC // 128
NTB = T // 512
NQB = T // 1024

VE_CH0 = 9
VE_CHK = 15
PE_CHK = 7


class Sem:
    def __init__(self, ctx, nc, name):
        self.h = ctx.enter_context(nc.semaphore(name))
        self.n = 0


def build_nc(dbg=False):
    nc = bass.Bass("TRN2", target_bir_lowering=False)

    def din(name, shape, dt=F32):
        return nc.dram_tensor(name, list(shape), dt, kind="ExternalInput").ap()

    x_c = din("x_c", [TC, D])
    xT_c = din("xT_c", [D, TC])
    cosg = din("cosg", [128, T])
    sing = din("sing", [128, T])
    masks = din("masks", [128, 12 * 1024], BF16)
    wkv_r = din("wkv_r", [16, 128, 128])
    wq_r = din("wq_r", [16, 128, 256])
    wout_r = din("wout_r", [16, 128, 16, 128])
    gw_r = din("gw_r", [NIT, 128, 16, 128])
    uw_r = din("uw_r", [NIT, 128, 16, 128])
    dw_r = din("dw_r", [16, 128, NIT, 128])
    gsc = din("gsc", [128, NIT])
    usc = din("usc", [128, NIT])
    dsc = din("dsc", [128, 16])
    sink = din("sink", [1, 4])
    pswap = din("pswap", [128, 128], BF16)
    ident = din("ident", [128, 128], BF16)
    onesr = din("onesr", [1, 128])
    konst = din("konst", [128, 4])

    out_d = nc.dram_tensor("out_d", [D, TC], F32, kind="ExternalOutput").ap()

    stage_xn = nc.dram_tensor("stage_xn", [NDT * NOC * 128, 128], BF16).ap()
    xn_all = nc.dram_tensor("xn_all", [NCORES * NDT * NOC * 128, 128], BF16).ap()
    q_d = nc.dram_tensor("q_d", [128, 2 * T], BF16).ap()
    hdn_d = nc.dram_tensor("hdn_d", [NIT, 128, 1024], BF16).ap()
    ctx_stage = nc.dram_tensor("ctx_stage", [D, TC], BF16).ap()
    ctx_all = nc.dram_tensor("ctx_all", [D, TC], BF16).ap()

    ctx = ExitStack()
    with ctx:
        sb = lambda n, s, d: ctx.enter_context(nc.sbuf_tensor(n, list(s), d))
        sem = lambda n: Sem(ctx, nc, n)
        block = ctx.enter_context(nc.Block())

        xn_T = sb("xn_T", [128, NOC * 2048], BF16)
        g_T = sb("g_T", [128, 16 * 1024], BF16)
        ctx_sb = sb("ctx_sb", [128, 2 * T], BF16)
        KT = sb("KT", [128, 4096], BF16)
        VS = sb("VS", [128, 64 * 65], BF16)
        gsc_sb = sb("gsc_sb", [128, NIT], F32)
        usc_sb = sb("usc_sb", [128, NIT], F32)
        dsc_sb = sb("dsc_sb", [128, 16], F32)
        pswap_sb = sb("pswap_sb", [128, 128], BF16)
        ident_sb = sb("ident_sb", [128, 128], BF16)
        ones_sb = sb("ones_sb", [1, 128], F32)
        sink_sb = sb("sink_sb", [1, 4], F32)
        esink_sb = sb("esink_sb", [1, 4], F32)
        konst_sb = sb("konst_sb", [128, 4], F32)

        LDC = sem("LDC")
        CCX = sem("CCX")
        CCA = sem("CCA")

        @block.gpsimd
        def _(g):
            for dst, src in ((gsc_sb, gsc), (usc_sb, usc), (dsc_sb, dsc),
                             (pswap_sb, pswap), (ident_sb, ident),
                             (ones_sb, onesr), (sink_sb, sink),
                             (konst_sb, konst)):
                g.dma_start(out=dst[:], in_=src).then_inc(LDC.h, 16)
                LDC.n += 16

        xnr = xn_T[:].rearrange("p (ct r) -> p ct r", ct=NOC)

        def xn_rhs(dt, n2):
            return xnr[:, 4 * n2:4 * n2 + 4, dt * 128:(dt + 1) * 128]

        with ExitStack() as p0:
            sb0 = lambda n, s, d: p0.enter_context(nc.sbuf_tensor(n, list(s), d))
            ps0 = lambda n, s, d: p0.enter_context(nc.psum_tensor(n, list(s), d))
            xch = [sb0(f"xch{i}", [128, 2048], F32) for i in range(2)]
            junk = sb0("junk0", [128, 2048], F32)
            junk2 = sb0("junk2", [128, 2048], F32)
            stat = sb0("stat", [128, 8], F32)
            xnm = sb0("xnm", [128, 2048], BF16)
            xnchT = [sb0(f"xnchT{i}", [128, 2048], BF16) for i in range(2)]
            xnT_ps = ps0("xnT_ps", [128, 2048], BF16)

            LDX = [sem("LDX0"), sem("LDX1")]
            VE0, PE0, SE0 = sem("VE0"), sem("PE0"), sem("SE0")
            STX = [sem("STX0"), sem("STX1")]

            @block.sync
            def _(sy):
                for ct in range(NOC):
                    if ct >= 2:
                        sy.wait_ge(VE0.h, (ct - 2) * VE_CH0 + 7)
                    sy.dma_start(out=xch[ct % 2][:],
                                 in_=x_c[ct * 128:(ct + 1) * 128, :]).then_inc(LDX[ct % 2].h, 16)
                    LDX[ct % 2].n += 16

            @block.scalar
            def _(s):
                s.wait_ge(LDC.h, LDC.n)
                for ct in range(NOC):
                    s.wait_ge(VE0.h, ct * VE_CH0 + 5)
                    ins = s.activation(stat[:, 7:8], stat[:, 4:5], AF.Ln,
                                       bias=konst_sb[:, 0:1])
                    ins.then_inc(SE0.h, 1); SE0.n += 1
                    s.wait_ge(SE0.h, ct * 2 + 1)
                    ins = s.activation(stat[:, 5:6], stat[:, 7:8], AF.Exp,
                                       scale=konst_sb[:, 1:2])
                    ins.then_inc(SE0.h, 1); SE0.n += 1

            @block.vector
            def _(v):
                def vdo(ins):
                    ins.then_inc(VE0.h, 1); VE0.n += 1
                    return ins
                for ct in range(NOC):
                    xc = xch[ct % 2]
                    v.wait_ge(LDX[ct % 2].h, (ct // 2 + 1) * 16)
                    vdo(v.tensor_scalar(junk[:], xc[:], 1.0, 0.0, op0=ALU.mult,
                                        op1=ALU.add, accum_out=stat[:, 0:1]))
                    vdo(v.scalar_tensor_tensor(junk2[:], xc[:], 1.0, xc[:], op0=ALU.mult,
                                               op1=ALU.mult, accum_out=stat[:, 1:2]))
                    v.wait_ge(VE0.h, VE0.n)
                    vdo(v.tensor_scalar(stat[:, 2:3], stat[:, 0:1], 1.0 / D, None, op0=ALU.mult))
                    v.wait_ge(VE0.h, VE0.n)
                    vdo(v.tensor_tensor(stat[:, 3:4], stat[:, 2:3], stat[:, 2:3], op=ALU.mult))
                    v.wait_ge(VE0.h, VE0.n)
                    vdo(v.scalar_tensor_tensor(stat[:, 4:5], stat[:, 1:2], 1.0 / D,
                                               stat[:, 3:4], op0=ALU.mult, op1=ALU.subtract))
                    v.wait_ge(SE0.h, (ct + 1) * 2)
                    vdo(v.tensor_scalar(stat[:, 6:7], stat[:, 2:3], stat[:, 5:6], -1.0,
                                        op0=ALU.mult, op1=ALU.mult))
                    v.wait_ge(VE0.h, VE0.n)
                    vdo(v.tensor_scalar(xnm[:], xc[:], stat[:, 5:6], stat[:, 6:7],
                                        op0=ALU.mult, op1=ALU.add))
                    v.wait_ge(PE0.h, ct + 1)
                    if ct >= 2:
                        v.wait_ge(STX[ct % 2].h, (ct // 2) * 16)
                    vdo(v.tensor_copy(xnchT[ct % 2][:], xnT_ps[:]))
                    vdo(v.tensor_copy(xn_T[:, ct * 2048:(ct + 1) * 2048], xnT_ps[:]))

            @block.tensor
            def _(t):
                t.wait_ge(LDC.h, LDC.n)
                for ct in range(NOC):
                    t.wait_ge(VE0.h, ct * VE_CH0 + 7)
                    if ct >= 1:
                        t.wait_ge(VE0.h, (ct - 1) * VE_CH0 + 9)
                    for dt in range(NDT):
                        ins = t.transpose(xnT_ps[:, dt * 128:(dt + 1) * 128],
                                          xnm[:, dt * 128:(dt + 1) * 128], ident_sb[:])
                    ins.then_inc(PE0.h, 1); PE0.n += 1

            sxr = stage_xn.rearrange("(dt ct p) t -> ct p dt t", dt=NDT, ct=NOC)

            @block.gpsimd
            def _(g):
                for ct in range(NOC):
                    g.wait_ge(VE0.h, ct * VE_CH0 + 8)
                    g.dma_start(out=sxr[ct],
                                in_=xnchT[ct % 2][:].rearrange(
                                    "p (dt t) -> p dt t", dt=NDT)).then_inc(STX[ct % 2].h, 16)
                    STX[ct % 2].n += 16
                g.wait_ge(STX[0].h, STX[0].n)
                g.wait_ge(STX[1].h, STX[1].n)
                g.collective_compute(
                    "AllGather", ALU.bypass,
                    replica_groups=[list(range(NCORES))],
                    ins=[stage_xn.tensor.ap().opt()],
                    outs=[xn_all.tensor.ap().opt()],
                ).then_inc(CCX.h)
                CCX.n = 1

            p0_final = dict(VE0=(VE0.h, VE0.n), PE0=(PE0.h, PE0.n),
                            SE0=(SE0.h, SE0.n), STX0=(STX[0].h, STX[0].n),
                            STX1=(STX[1].h, STX[1].n))

        with ExitStack() as p1:
            sb1 = lambda n, s, d: p1.enter_context(nc.sbuf_tensor(n, list(s), d))
            ps1 = lambda n, s, d: p1.enter_context(nc.psum_tensor(n, list(s), d))
            gw_sb = [sb1(f"gw{i}", [128, 2048], BF16) for i in range(2)]
            uw_sb = [sb1(f"uw{i}", [128, 2048], BF16) for i in range(2)]
            sil = [sb1(f"sil{i}", [128, 1024], F32) for i in range(2)]
            hdnst = [sb1(f"hdnst{i}", [128, 1024], BF16) for i in range(2)]
            g_ps = [ps1(f"g_ps{i}", [128, 1024], F32) for i in range(2)]
            u_ps = [ps1(f"u_ps{i}", [128, 1024], F32) for i in range(2)]

            LD1 = [sem("LD1a"), sem("LD1b")]
            STH = [sem("STHa"), sem("STHb")]
            PE1, VE1, SE1 = sem("PE1"), sem("VE1"), sem("SE1")

            def p1a_store(g, it):
                g.wait_ge(VE1.h, it + 1)
                g.dma_start(out=hdn_d[it], in_=hdnst[it % 2][:]).then_inc(STH[it % 2].h, 16)
                STH[it % 2].n += 16

            @block.gpsimd
            def _(g):
                for hs, vv in p0_final.items():
                    g.wait_ge(vv[0], vv[1])
                for it in range(NIT):
                    if it >= 2:
                        g.wait_ge(PE1.h, (it - 1) * 2)
                    g.dma_start(out=gw_sb[it % 2][:].rearrange("p (dt c) -> p dt c", dt=16),
                                in_=gw_r[it]).then_inc(LD1[it % 2].h, 16)
                    LD1[it % 2].n += 16
                    g.dma_start(out=uw_sb[it % 2][:].rearrange("p (dt c) -> p dt c", dt=16),
                                in_=uw_r[it]).then_inc(LD1[it % 2].h, 16)
                    LD1[it % 2].n += 16
                    if it >= 2:
                        p1a_store(g, it - 2)
                p1a_store(g, NIT - 2)
                p1a_store(g, NIT - 1)

            @block.tensor
            def _(t):
                for hs, vv in p0_final.items():
                    t.wait_ge(vv[0], vv[1])
                for it in range(NIT):
                    t.wait_ge(LD1[it % 2].h, (it // 2 + 1) * 32)
                    if it >= 2:
                        t.wait_ge(SE1.h, it - 1)
                    for dt in range(NDT):
                        for n2 in range(2):
                            ins = t.matmul(g_ps[it % 2][:, n2 * 512:(n2 + 1) * 512],
                                           gw_sb[it % 2][:, dt * 128:(dt + 1) * 128],
                                           xn_rhs(dt, n2),
                                           start=(dt == 0), stop=(dt == NDT - 1))
                    ins.then_inc(PE1.h, 1); PE1.n += 1
                    if it >= 2:
                        t.wait_ge(VE1.h, it - 1)
                    for dt in range(NDT):
                        for n2 in range(2):
                            ins = t.matmul(u_ps[it % 2][:, n2 * 512:(n2 + 1) * 512],
                                           uw_sb[it % 2][:, dt * 128:(dt + 1) * 128],
                                           xn_rhs(dt, n2),
                                           start=(dt == 0), stop=(dt == NDT - 1))
                    ins.then_inc(PE1.h, 1); PE1.n += 1

            @block.scalar
            def _(s):
                for it in range(NIT):
                    s.wait_ge(PE1.h, it * 2 + 1)
                    if it >= 2:
                        s.wait_ge(VE1.h, it - 1)
                    ins = s.activation(sil[it % 2][:], g_ps[it % 2][:], AF.Silu,
                                       scale=gsc_sb[:, it:it + 1])
                    ins.then_inc(SE1.h, 1); SE1.n += 1

            @block.vector
            def _(v):
                for it in range(NIT):
                    v.wait_ge(PE1.h, it * 2 + 2)
                    v.wait_ge(SE1.h, it + 1)
                    if it >= 2:
                        v.wait_ge(STH[it % 2].h, (it // 2) * 16)
                    ins = v.scalar_tensor_tensor(hdnst[it % 2][:], u_ps[it % 2][:],
                                                 usc_sb[:, it:it + 1], sil[it % 2][:],
                                                 op0=ALU.mult, op1=ALU.mult)
                    ins.then_inc(VE1.h, 1); VE1.n += 1

            p1a_final = dict(PE1=(PE1.h, PE1.n), VE1=(VE1.h, VE1.n),
                             SE1=(SE1.h, SE1.n), STHa=(STH[0].h, STH[0].n),
                             STHb=(STH[1].h, STH[1].n))

        with ExitStack() as p1b:
            sb2 = lambda n, s, d: p1b.enter_context(nc.sbuf_tensor(n, list(s), d))
            ps2 = lambda n, s, d: p1b.enter_context(nc.psum_tensor(n, list(s), d))
            dw_sb = [sb2(f"dw{i}", [128, NIT * 128], BF16) for i in range(2)]
            hdn_half = sb2("hdn_half", [128, NIT * 512], BF16)
            d_ps = [ps2(f"d_ps{i}", [128, 512], F32) for i in range(2)]

            LD2 = [sem("LD2a"), sem("LD2b")]
            LDH = sem("LDH")
            PE2, SE2 = sem("PE2"), sem("SE2")
            hdn_r = hdn_d.rearrange("it p t -> p it t")

            @block.sync
            def _(sy):
                for hs, vv in p1a_final.items():
                    sy.wait_ge(vv[0], vv[1])
                for th in range(2):
                    if th == 1:
                        sy.wait_ge(PE2.h, 16)
                    src = hdn_r[:, :, th * 512:(th + 1) * 512]
                    for q in range(4):
                        sy.dma_start(out=hdn_half[:, q * 11 * 512:(q + 1) * 11 * 512]
                                     .rearrange("p (it t) -> p it t", it=11),
                                     in_=src[:, q * 11:(q + 1) * 11, :]).then_inc(LDH.h, 16)
                        LDH.n += 16

            @block.gpsimd
            def _(g):
                for hs, vv in p1a_final.items():
                    g.wait_ge(vv[0], vv[1])
                for it2 in range(32):
                    dmt = it2 % 16
                    if it2 >= 2:
                        g.wait_ge(PE2.h, it2 - 1)
                    g.dma_start(out=dw_sb[it2 % 2][:].rearrange("p (it c) -> p it c", it=NIT),
                                in_=dw_r[dmt]).then_inc(LD2[it2 % 2].h, 16)
                    LD2[it2 % 2].n += 16

            @block.tensor
            def _(t):
                for hs, vv in p1a_final.items():
                    t.wait_ge(vv[0], vv[1])
                for it2 in range(32):
                    th, dmt = it2 // 16, it2 % 16
                    t.wait_ge(LD2[it2 % 2].h, (it2 // 2 + 1) * 16)
                    t.wait_ge(LDH.h, (th + 1) * 64)
                    if it2 >= 2:
                        t.wait_ge(SE2.h, it2 - 1)
                    for it in range(NIT):
                        ins = t.matmul(d_ps[it2 % 2][:],
                                       dw_sb[it2 % 2][:, it * 128:(it + 1) * 128],
                                       hdn_half[:, it * 512:(it + 1) * 512],
                                       start=(it == 0), stop=(it == NIT - 1))
                    ins.then_inc(PE2.h, 1); PE2.n += 1

            @block.scalar
            def _(s):
                for it2 in range(32):
                    th, dmt = it2 // 16, it2 % 16
                    s.wait_ge(PE2.h, it2 + 1)
                    ins = s.activation(g_T[:, dmt * 1024 + th * 512:dmt * 1024 + th * 512 + 512],
                                       d_ps[it2 % 2][:], AF.Silu, scale=dsc_sb[:, dmt:dmt + 1])
                    ins.then_inc(SE2.h, 1); SE2.n += 1

            p1b_final = dict(PE2=(PE2.h, PE2.n), SE2=(SE2.h, SE2.n))

        with ExitStack() as pk:
            sbk = lambda n, s, d: pk.enter_context(nc.sbuf_tensor(n, list(s), d))
            psk = lambda n, s, d: pk.enter_context(nc.psum_tensor(n, list(s), d))
            xnW = [sbk(f"xnW{i}", [128, NDT * 512], BF16) for i in range(2)]
            wkv_sb = sbk("wkv_sb", [128, 16 * 128], BF16)
            wq_sb = sbk("wq_sb", [128, 16 * 256], BF16)
            cosQ = [sbk(f"cosQ{i}", [128, 512], F32) for i in range(2)]
            sinQ = [sbk(f"sinQ{i}", [128, 512], F32) for i in range(2)]
            kbf = sbk("kbf", [64, 512], BF16)
            vbf = sbk("vbf", [64, 512], BF16)
            t1s = sbk("t1s", [64, 512], F32)
            t2s = sbk("t2s", [64, 512], F32)
            qbf = sbk("qbf", [128, 512], BF16)
            t1q = sbk("t1q", [128, 512], F32)
            t2q = sbk("t2q", [128, 512], F32)
            qstage = [sbk(f"qstage{i}", [128, 512], BF16) for i in range(2)]
            kvps = psk("kvps", [128, 512], F32)
            ksw = psk("ksw", [64, 512], F32)
            vT_ps = psk("vT_ps", [128, 256], BF16)
            qp = [psk(f"qp{i}", [128, 512], F32) for i in range(2)]
            qsw = psk("qsw", [128, 512], F32)

            LXW = [sem("LXWa"), sem("LXWb")]
            LWV, LRT = sem("LWV"), sem("LRT")
            PEK, VEK = sem("PEK"), sem("VEK")
            STQ = [sem("STQa"), sem("STQb")]

            xar = xn_all.rearrange("(rk dt ct p) t -> rk dt ct p t",
                                   rk=NCORES, dt=NDT, ct=NOC)

            @block.gpsimd
            def _(g):
                for hs, vv in p1b_final.items():
                    g.wait_ge(vv[0], vv[1])
                g.dma_start(out=wkv_sb[:].rearrange("p (dt c) -> p dt c", dt=16),
                            in_=wkv_r.rearrange("dt p c -> p dt c")).then_inc(LWV.h, 16)
                g.dma_start(out=wq_sb[:].rearrange("p (dt c) -> p dt c", dt=16),
                            in_=wq_r.rearrange("dt p c -> p dt c")).then_inc(LWV.h, 16)
                LWV.n += 32

            @block.sync
            def _(sy):
                for hs, vv in p1b_final.items():
                    sy.wait_ge(vv[0], vv[1])
                sy.wait_ge(CCX.h, 1)
                for tb in range(NTB):
                    rk, half = tb // 2, tb % 2
                    if tb >= 2:
                        sy.wait_ge(PEK.h, (tb - 1) * PE_CHK)
                    for dt in range(NDT):
                        src = xar[rk, dt, 4 * half:4 * half + 4, :, :]
                        sy.dma_start(
                            out=xnW[tb % 2][:, dt * 512:(dt + 1) * 512],
                            in_=src.rearrange("ct p t -> p ct t"),
                        ).then_inc(LXW[tb % 2].h, 16)
                        LXW[tb % 2].n += 16
                    if tb >= 1:
                        sy.wait_ge(VEK.h, (tb - 1) * VE_CHK + 1)
                    sy.dma_start(out=cosQ[tb % 2][:],
                                 in_=cosg[:, tb * 512:(tb + 1) * 512]).then_inc(LRT.h, 16)
                    sy.dma_start(out=sinQ[tb % 2][:],
                                 in_=sing[:, tb * 512:(tb + 1) * 512]).then_inc(LRT.h, 16)
                    LRT.n += 32
                    if tb >= 1:
                        for mp in range(2):
                            sy.wait_ge(VEK.h, (tb - 1) * VE_CHK + 11 + 4 * mp)
                            sy.dma_start(out=q_d[:, mp * T + (tb - 1) * 512:mp * T + tb * 512],
                                         in_=qstage[mp][:]).then_inc(STQ[mp].h, 16)
                            STQ[mp].n += 16
                for mp in range(2):
                    sy.wait_ge(VEK.h, (NTB - 1) * VE_CHK + 11 + 4 * mp)
                    sy.dma_start(out=q_d[:, mp * T + (NTB - 1) * 512:mp * T + NTB * 512],
                                 in_=qstage[mp][:]).then_inc(STQ[mp].h, 16)
                    STQ[mp].n += 16

            @block.tensor
            def _(t):
                for hs, vv in p1b_final.items():
                    t.wait_ge(vv[0], vv[1])
                t.wait_ge(LWV.h, 32)
                for tb in range(NTB):
                    t.wait_ge(LXW[tb % 2].h, (tb // 2 + 1) * 256)
                    xw = xnW[tb % 2]
                    if tb >= 1:
                        t.wait_ge(VEK.h, (tb - 1) * VE_CHK + 4)
                    for dt in range(NDT):
                        ins = t.matmul(kvps[:],
                                       wkv_sb[:, dt * 128:(dt + 1) * 128],
                                       xw[:, dt * 512:(dt + 1) * 512],
                                       start=(dt == 0), stop=(dt == NDT - 1))
                    ins.then_inc(PEK.h, 1); PEK.n += 1
                    t.wait_ge(VEK.h, tb * VE_CHK + 1)
                    if tb >= 1:
                        t.wait_ge(VEK.h, (tb - 1) * VE_CHK + 3)
                    ins = t.matmul(ksw[:], pswap_sb[0:64, 0:64], kbf[:],
                                   start=True, stop=True)
                    ins.then_inc(PEK.h, 1); PEK.n += 1
                    t.wait_ge(VEK.h, tb * VE_CHK + 2)
                    if tb >= 1:
                        t.wait_ge(VEK.h, (tb - 1) * VE_CHK + 7)
                    for gi in range(4):
                        ins = t.transpose(vT_ps[:, gi * 64:(gi + 1) * 64],
                                          vbf[:, gi * 128:(gi + 1) * 128],
                                          ident_sb[0:64, 0:64])
                    ins.then_inc(PEK.h, 1); PEK.n += 1
                    for mp in range(2):
                        if tb >= 1:
                            t.wait_ge(VEK.h, (tb - 1) * VE_CHK + 10 + 4 * mp)
                        for dt in range(NDT):
                            ins = t.matmul(qp[mp][:],
                                           wq_sb[:, dt * 256 + mp * 128:dt * 256 + (mp + 1) * 128],
                                           xw[:, dt * 512:(dt + 1) * 512],
                                           start=(dt == 0), stop=(dt == NDT - 1))
                        ins.then_inc(PEK.h, 1); PEK.n += 1
                        t.wait_ge(VEK.h, tb * VE_CHK + 8 + mp * 4)
                        if tb >= 1 and mp == 0:
                            t.wait_ge(VEK.h, (tb - 1) * VE_CHK + 13)
                        if mp == 1:
                            t.wait_ge(VEK.h, tb * VE_CHK + 9)
                        ins = t.matmul(qsw[:], pswap_sb[:], qbf[:],
                                       start=True, stop=True)
                        ins.then_inc(PEK.h, 1); PEK.n += 1

            @block.vector
            def _(v):
                def vdo(ins):
                    ins.then_inc(VEK.h, 1); VEK.n += 1
                    return ins
                v.memset(VS[:].rearrange("k (g c) -> k g c", c=65)[:, :, 64:65], 1.0)
                for hs, vv in p1b_final.items():
                    v.wait_ge(vv[0], vv[1])
                for tb in range(NTB):
                    base = tb * VE_CHK
                    v.wait_ge(LRT.h, (tb + 1) * 32)
                    v.wait_ge(PEK.h, tb * PE_CHK + 1)
                    vdo(v.tensor_copy(kbf[:], kvps[0:64, :]))
                    vdo(v.tensor_copy(vbf[:], kvps[64:128, :]))
                    v.wait_ge(PEK.h, tb * PE_CHK + 2)
                    vdo(v.tensor_tensor(t1s[:], ksw[:], sinQ[tb % 2][0:64, :], op=ALU.mult))
                    vdo(v.tensor_tensor(t2s[:], kvps[0:64, :], cosQ[tb % 2][0:64, :], op=ALU.mult))
                    v.wait_ge(VEK.h, VEK.n)
                    t1v = t1s[:].rearrange("p (gi2 two k) -> p two gi2 k", two=2, k=128)
                    t2v = t2s[:].rearrange("p (gi2 two k) -> p two gi2 k", two=2, k=128)
                    ktv = KT[:].rearrange("p (b k) -> p b k", k=128)
                    vdo(v.tensor_tensor(ktv[0:64, 2 * tb:2 * tb + 2, :],
                                        t1v[:, 0], t2v[:, 0], op=ALU.add))
                    vdo(v.tensor_tensor(ktv[64:128, 2 * tb:2 * tb + 2, :],
                                        t1v[:, 1], t2v[:, 1], op=ALU.add))
                    v.wait_ge(PEK.h, tb * PE_CHK + 3)
                    vdo(v.tensor_copy(
                        VS[:].rearrange("k (g c) -> k g c", c=65)
                        [:, 4 * tb:4 * tb + 4, 0:64],
                        vT_ps[:].rearrange("k (g c) -> k g c", c=64)))
                    for mp in range(2):
                        v.wait_ge(PEK.h, tb * PE_CHK + 4 + mp * 2)
                        vdo(v.tensor_copy(qbf[:], qp[mp][:]))
                        v.wait_ge(PEK.h, tb * PE_CHK + 5 + mp * 2)
                        vdo(v.tensor_tensor(t1q[:], qsw[:], sinQ[tb % 2][:], op=ALU.mult))
                        vdo(v.tensor_tensor(t2q[:], qp[mp][:], cosQ[tb % 2][:], op=ALU.mult))
                        v.wait_ge(VEK.h, VEK.n)
                        if tb >= 1:
                            v.wait_ge(STQ[mp].h, tb * 16)
                        vdo(v.tensor_tensor(qstage[mp][:], t1q[:], t2q[:], op=ALU.add))

            pkv_final = dict(PEK=(PEK.h, PEK.n), VEK=(VEK.h, VEK.n),
                             STQa=(STQ[0].h, STQ[0].n), STQb=(STQ[1].h, STQ[1].n))

        with ExitStack() as p2:
            sb3 = lambda n, s, d: p2.enter_context(nc.sbuf_tensor(n, list(s), d))
            ps3 = lambda n, s, d: p2.enter_context(nc.psum_tensor(n, list(s), d))
            masks_sb = sb3("masks_sb", [128, 12 * 1024], BF16)
            QS = [sb3(f"QS{i}", [128, 1024], BF16) for i in range(2)]
            ES = [sb3(f"ES{i}", [128, 1024], BF16) for i in range(4)]
            dsum = sb3("dsum", [1, 1024], F32)
            rbc = sb3("rbc", [128, 1024], F32)
            scps = [ps3(f"scps{i}", [128, 1024], F32) for i in range(3)]
            ctxp = ps3("ctxp", [65, 1024], F32)

            LDM = sem("LDM")
            LQS = [sem("LQSa"), sem("LQSb")]
            QKs, EXs, MKs, WVs, BCs, VE4, SE4 = (sem("QKs"), sem("EXs"), sem("MKs"),
                                                 sem("WVs"), sem("BCs"), sem("VE4"),
                                                 sem("SE4"))

            def masked_pi(qb, sp):
                pi = sp - 4 * qb + 2
                return pi if 0 <= pi < 6 else None

            mk_after = {}
            _cnt = 0
            for _ob in range(32):
                _m, _qb = _ob // NQB, _ob % NQB
                for _sp in range(32):
                    for _par in range(2):
                        _j = _ob * 64 + _sp * 2 + _par
                        if masked_pi(_qb, _sp) is not None:
                            _cnt += 1
                        mk_after[_j] = _cnt

            @block.gpsimd
            def _(g):
                for hs, vv in pkv_final.items():
                    g.wait_ge(vv[0], vv[1])
                g.dma_start(out=masks_sb[:], in_=masks).then_inc(LDM.h, 16)
                LDM.n += 16

            @block.scalar
            def _(s):
                s.wait_ge(LDC.h, LDC.n)
                ins = s.activation(esink_sb[:], sink_sb[:], AF.Exp)
                ins.then_inc(SE4.h, 1); SE4.n += 1

            @block.sync
            def _(sy):
                for hs, vv in pkv_final.items():
                    sy.wait_ge(vv[0], vv[1])
                for ob in range(32):
                    m, qb = ob // NQB, ob % NQB
                    if ob >= 2:
                        sy.wait_ge(QKs.h, (ob - 1) * 32)
                    half = q_d[(m % 2) * 64:(m % 2) * 64 + 64,
                               (m // 2) * T + qb * 1024:(m // 2) * T + (qb + 1) * 1024]
                    sy.dma_start(out=QS[ob % 2][0:64, :], in_=half).then_inc(LQS[ob % 2].h, 16)
                    sy.dma_start(out=QS[ob % 2][64:128, :], in_=half).then_inc(LQS[ob % 2].h, 16)
                    LQS[ob % 2].n += 32

            @block.tensor
            def _(t):
                for hs, vv in pkv_final.items():
                    t.wait_ge(vv[0], vv[1])

                def emit_wv(ob, sp, par):
                    j = ob * 64 + sp * 2 + par
                    qb = ob % NQB
                    if masked_pi(qb, sp) is not None:
                        t.wait_ge(MKs.h, mk_after[j])
                    else:
                        t.wait_ge(EXs.h, j + 1)
                    if sp == 0 and par == 0:
                        t.wait_ge(VE4.h, ob * 4)
                    ls = sp * 2 + par
                    for qh in range(2):
                        ins = t.matmul(ctxp[:, qh * 512:(qh + 1) * 512],
                                       VS[:, ls * 65:ls * 65 + 65],
                                       ES[j % 4][:, qh * 512:(qh + 1) * 512],
                                       start=(sp == 0 and par == 0),
                                       stop=(sp == 31 and par == 1))
                    ins.then_inc(WVs.h, 1); WVs.n += 1

                for ob in range(32):
                    m, qb = ob // NQB, ob % NQB
                    t.wait_ge(LQS[ob % 2].h, (ob // 2 + 1) * 32)
                    for sp in range(32):
                        j0 = ob * 64 + sp * 2
                        if j0 >= 2:
                            t.wait_ge(EXs.h, j0 - 1)
                        if sp <= 1:
                            if ob >= 1:
                                t.wait_ge(VE4.h, (ob - 1) * 4 + 3)
                        for qh in range(2):
                            t.matmul(scps[j0 % 3][:, qh * 512:(qh + 1) * 512],
                                     KT[0:64, sp * 128:(sp + 1) * 128],
                                     QS[ob % 2][0:64, qh * 512:(qh + 1) * 512],
                                     start=True, stop=True, tile_position=(0, 0))
                            ins = t.matmul(scps[(j0 + 1) % 3][:, qh * 512:(qh + 1) * 512],
                                           KT[64:128, sp * 128:(sp + 1) * 128],
                                           QS[ob % 2][64:128, qh * 512:(qh + 1) * 512],
                                           start=True, stop=True, tile_position=(64, 0))
                        ins.then_inc(QKs.h, 1); QKs.n += 1
                        if sp >= 1:
                            emit_wv(ob, sp - 1, 0)
                            emit_wv(ob, sp - 1, 1)
                    emit_wv(ob, 31, 0)
                    emit_wv(ob, 31, 1)
                    t.wait_ge(VE4.h, ob * 4 + 2)
                    t.wait_ge(EXs.h, ob * 64 + 62)
                    bb = (ob * 64 + 64) % 3
                    for qh in range(2):
                        ins = t.matmul(scps[bb][:, qh * 512:(qh + 1) * 512],
                                       ones_sb[:],
                                       dsum[0:1, qh * 512:(qh + 1) * 512],
                                       start=True, stop=True)
                    ins.then_inc(BCs.h, 1); BCs.n += 1

            @block.scalar
            def _(s):
                for ob in range(32):
                    for sp in range(32):
                        for par in range(2):
                            j = ob * 64 + sp * 2 + par
                            s.wait_ge(QKs.h, ob * 32 + sp + 1)
                            if j >= 4:
                                s.wait_ge(WVs.h, j - 3)
                            ins = s.activation(ES[j % 4][:], scps[j % 3][:],
                                               AF.Exp, scale=konst_sb[:, 2:3])
                            ins.then_inc(EXs.h, 1); EXs.n += 1

            @block.vector
            def _(v):
                def vdo4(ins):
                    ins.then_inc(VE4.h, 1); VE4.n += 1
                    return ins
                v.wait_ge(SE4.h, 1)
                v.wait_ge(LDM.h, 16)
                for ob in range(32):
                    m, qb = ob // NQB, ob % NQB
                    for sp in range(32):
                        pi = masked_pi(qb, sp)
                        if pi is None:
                            continue
                        for par in range(2):
                            j = ob * 64 + sp * 2 + par
                            v.wait_ge(EXs.h, j + 1)
                            sidx = pi * 2 + par
                            ins = v.tensor_tensor(ES[j % 4][:], ES[j % 4][:],
                                                  masks_sb[:, sidx * 1024:(sidx + 1) * 1024],
                                                  op=ALU.mult)
                            ins.then_inc(MKs.h, 1); MKs.n += 1
                    v.wait_ge(WVs.h, (ob + 1) * 64)
                    vdo4(v.tensor_copy(dsum[:], ctxp[64:65, :]))
                    v.wait_ge(VE4.h, VE4.n)
                    vdo4(v.tensor_scalar(dsum[:], dsum[:],
                                         esink_sb[0:1, m:m + 1],
                                         None, op0=ALU.add))
                    v.wait_ge(BCs.h, ob + 1)
                    bb = (ob * 64 + 64) % 3
                    vdo4(v.reciprocal(rbc[:], scps[bb][:]))
                    v.wait_ge(VE4.h, VE4.n)
                    vdo4(v.tensor_tensor(
                        ctx_sb[(m % 2) * 64:(m % 2) * 64 + 64,
                               (m // 2) * T + qb * 1024:(m // 2) * T + (qb + 1) * 1024],
                        ctxp[0:64, :], rbc[0:64, :], op=ALU.mult))

            p2_final = dict(QKs=(QKs.h, QKs.n), EXs=(EXs.h, EXs.n),
                            WVs=(WVs.h, WVs.n), VE4=(VE4.h, VE4.n),
                            BCs=(BCs.h, BCs.n), MKs=(MKs.h, MKs.n))

        STC = sem("STC")

        @block.gpsimd
        def _(g):
            for hs, vv in p2_final.items():
                g.wait_ge(vv[0], vv[1])
            csr = ctx_stage.rearrange("(j rg p) t -> rg p j t", j=NCORES, rg=2)
            cbr = ctx_sb[:].rearrange("p (rg j t) -> p rg j t", rg=2, j=NCORES)
            for rg in range(2):
                g.dma_start(out=csr[rg], in_=cbr[:, rg]).then_inc(STC.h, 16)
                STC.n += 16
            g.wait_ge(STC.h, STC.n)
            with allow_ici_all_to_all():
                g.collective_compute(
                    "AllToAll", ALU.bypass,
                    replica_groups=[list(range(NCORES))],
                    ins=[ctx_stage.tensor.ap().opt()],
                    outs=[ctx_all.tensor.ap().opt()],
                ).then_inc(CCA.h)
            CCA.n = 1

        with ExitStack() as p3:
            sb4 = lambda n, s, d: p3.enter_context(nc.sbuf_tensor(n, list(s), d))
            ps4 = lambda n, s, d: p3.enter_context(nc.psum_tensor(n, list(s), d))
            ctxR = sb4("ctxR", [128, 16 * 1024], BF16)
            wo_sb = [sb4(f"wo{i}", [128, 2048], BF16) for i in range(2)]
            xt_sb = [sb4(f"xt{i}", [128, 1024], F32) for i in range(2)]
            prod = sb4("prod", [128, 1024], F32)
            outst = [sb4(f"outst{i}", [128, 1024], F32) for i in range(2)]
            s_ps = [ps4(f"s_ps{i}", [128, 1024], F32) for i in range(2)]

            LD5 = [sem("LD5a"), sem("LD5b")]
            LCX = sem("LCX")
            LX3 = [sem("LX3a"), sem("LX3b")]
            STO = [sem("STOa"), sem("STOb")]
            PE5, VE5 = sem("PE5"), sem("VE5")

            @block.gpsimd
            def _(g):
                g.wait_ge(CCA.h, 1)
                for dv in range(16):
                    g.dma_start(out=ctxR[:, dv * 1024:(dv + 1) * 1024],
                                in_=ctx_all[dv * 128:(dv + 1) * 128, :]).then_inc(LCX.h, 16)
                    LCX.n += 16
                for dmt in range(16):
                    if dmt >= 2:
                        g.wait_ge(PE5.h, dmt - 1)
                    g.dma_start(out=wo_sb[dmt % 2][:].rearrange("p (dv c) -> p dv c", dv=16),
                                in_=wout_r[dmt]).then_inc(LD5[dmt % 2].h, 16)
                    LD5[dmt % 2].n += 16

            def p3_store(sy, dmt):
                sy.wait_ge(VE5.h, dmt * 2 + 2)
                sy.dma_start(out=out_d[dmt * 128:(dmt + 1) * 128, :],
                             in_=outst[dmt % 2][:]).then_inc(STO[dmt % 2].h, 16)
                STO[dmt % 2].n += 16

            @block.sync
            def _(sy):
                for hs, vv in p2_final.items():
                    sy.wait_ge(vv[0], vv[1])
                for dmt in range(16):
                    if dmt >= 2:
                        sy.wait_ge(VE5.h, (dmt - 1) * 2)
                    sy.dma_start(out=xt_sb[dmt % 2][:],
                                 in_=xT_c[dmt * 128:(dmt + 1) * 128, :]).then_inc(LX3[dmt % 2].h, 16)
                    LX3[dmt % 2].n += 16
                    if dmt >= 1:
                        p3_store(sy, dmt - 1)
                p3_store(sy, 15)
                sy.wait_ge(STO[0].h, 128)
                sy.wait_ge(STO[1].h, 128)

            @block.tensor
            def _(t):
                t.wait_ge(LCX.h, 256)
                for dmt in range(16):
                    t.wait_ge(LD5[dmt % 2].h, (dmt // 2 + 1) * 16)
                    if dmt >= 2:
                        t.wait_ge(VE5.h, (dmt - 1) * 2)
                    for dv in range(16):
                        for n2 in range(2):
                            ins = t.matmul(s_ps[dmt % 2][:, n2 * 512:(n2 + 1) * 512],
                                           wo_sb[dmt % 2][:, dv * 128:(dv + 1) * 128],
                                           ctxR[:, dv * 1024 + n2 * 512:dv * 1024 + (n2 + 1) * 512],
                                           start=(dv == 0), stop=(dv == 15))
                    ins.then_inc(PE5.h, 1); PE5.n += 1

            @block.vector
            def _(v):
                for hs, vv in p1b_final.items():
                    v.wait_ge(vv[0], vv[1])
                for dmt in range(16):
                    v.wait_ge(PE5.h, dmt + 1)
                    v.wait_ge(VE5.h, VE5.n)
                    ins = v.tensor_tensor(prod[:], s_ps[dmt % 2][:],
                                          g_T[:, dmt * 1024:(dmt + 1) * 1024], op=ALU.mult)
                    ins.then_inc(VE5.h, 1); VE5.n += 1
                    v.wait_ge(VE5.h, VE5.n)
                    v.wait_ge(LX3[dmt % 2].h, (dmt // 2 + 1) * 16)
                    if dmt >= 2:
                        v.wait_ge(STO[dmt % 2].h, (dmt // 2) * 16)
                    ins = v.tensor_tensor(outst[dmt % 2][:], prod[:], xt_sb[dmt % 2][:],
                                          op=ALU.add)
                    ins.then_inc(VE5.h, 1); VE5.n += 1

    return nc


def _rope_tables():
    import os, subprocess, tempfile
    prog = (
        "import numpy as np, jax.numpy as jnp, sys\n"
        "inv = (1.0 / 1024.0) ** jnp.linspace(0.0, 1.0, %d)\n"
        "inv = jnp.concatenate([inv, jnp.zeros(%d)])\n"
        "th = jnp.arange(%d, dtype=jnp.float32)[:, None] * inv[None, :]\n"
        "np.save(sys.argv[1] + 'c.npy', np.asarray(jnp.cos(th), np.float32))\n"
        "np.save(sys.argv[1] + 's.npy', np.asarray(jnp.sin(th), np.float32))\n"
        % (HD // 4, HD // 4, T))
    try:
        with tempfile.NamedTemporaryFile(suffix="_rope", delete=False) as f:
            path = f.name
        env = dict(os.environ)
        env.pop("TRN_TERMINAL_POOL_IPS", None)
        env["JAX_PLATFORMS"] = "cpu"
        npp = env.get("NIX_PYTHONPATH", "")
        if npp:
            env["PYTHONPATH"] = npp + os.pathsep + env.get("PYTHONPATH", "")
        r = subprocess.run([sys.executable, "-c", prog, path], env=env,
                           capture_output=True, timeout=600)
        if r.returncode == 0:
            return (np.load(path + "c.npy"), np.load(path + "s.npy"))
    except Exception:
        pass
    lin = np.linspace(0.0, 1.0, HD // 4).astype(np.float32)
    inv = np.concatenate([np.float32(1.0 / 1024.0) ** lin,
                          np.zeros(HD // 4, np.float32)])
    th = np.arange(T, dtype=np.float32)[:, None] * inv[None, :]
    return np.cos(th.astype(np.float64)).astype(np.float32), \
        np.sin(th.astype(np.float64)).astype(np.float32)


def _host_prep(L0_attn_x, L0_attn_W_qkv, L0_attn_W_out, L0_attn_s,
               L0_ffn_gate_w, L0_ffn_up_w, L0_ffn_down_w,
               L0_ffn_gate_s, L0_ffn_up_s, L0_ffn_down_s):
    bf = ml_dtypes.bfloat16
    x = np.ascontiguousarray(np.asarray(L0_attn_x, np.float32).reshape(T, D))
    Wqkv = np.asarray(L0_attn_W_qkv, np.float32)
    Wout = np.asarray(L0_attn_W_out, np.float32)
    s_in = np.asarray(L0_attn_s, np.float32)
    gate_w = np.asarray(L0_ffn_gate_w, np.float32)
    up_w = np.asarray(L0_ffn_up_w, np.float32)
    down_w = np.asarray(L0_ffn_down_w, np.float32)
    gate_s = np.asarray(L0_ffn_gate_s, np.float32)
    up_s = np.asarray(L0_ffn_up_s, np.float32)
    down_s = np.asarray(L0_ffn_down_s, np.float32)

    shared = {}
    shared["wout_r"] = np.ascontiguousarray(
        Wout.reshape(16, 128, 16, 128).transpose(2, 1, 0, 3))
    shared["gw_r"] = np.ascontiguousarray(
        gate_w.reshape(NIT, 128, 16, 128).transpose(0, 3, 2, 1))
    shared["uw_r"] = np.ascontiguousarray(
        up_w.reshape(NIT, 128, 16, 128).transpose(0, 3, 2, 1))
    shared["dw_r"] = np.ascontiguousarray(
        down_w.reshape(16, 128, NIT, 128).transpose(0, 3, 2, 1))
    shared["gsc"] = np.ascontiguousarray(gate_s.reshape(NIT, 128).T)
    shared["usc"] = np.ascontiguousarray(up_s.reshape(NIT, 128).T)
    shared["dsc"] = np.ascontiguousarray(down_s.reshape(16, 128).T)
    PS = np.zeros((128, 128), np.float32)
    for b in range(2):
        B = 64 * b
        for i in range(32):
            PS[B + 32 + i, B + i] = 1.0
            PS[B + i, B + 32 + i] = -1.0
    shared["pswap"] = PS.astype(bf)
    shared["ident"] = np.eye(128, dtype=bf)
    shared["onesr"] = np.ones((1, 128), np.float32)
    shared["konst"] = np.broadcast_to(
        np.array([1e-5, -0.5, 0.125, 0.0], np.float32), (128, 4)).copy()

    cos, sin = _rope_tables()
    shared["cosg"] = np.ascontiguousarray(np.tile(cos.T, (4, 1)))
    shared["sing"] = np.ascontiguousarray(np.tile(sin.T, (4, 1)))

    mk = np.ones((6, 2, 128, 1024), np.float32)
    tq = np.arange(1024)
    tk = np.arange(128)
    for pi in range(6):
        for par in range(2):
            dlt = 2 * (pi - 2) + par
            diff = tq[None, :] - 128 * dlt - tk[:, None]
            bad = (diff >= 0) & (diff <= 511)
            mk[pi, par] = np.where(bad, 0.0, 1.0)
    shared["masks"] = np.ascontiguousarray(
        mk.reshape(12, 128, 1024).transpose(1, 0, 2).reshape(128, 12 * 1024)).astype(bf)

    in_maps = []
    for c in range(NCORES):
        m = dict(shared)
        xc = np.ascontiguousarray(x[TC * c:TC * (c + 1)])
        m["x_c"] = xc
        m["xT_c"] = np.ascontiguousarray(xc.T)
        wk = Wqkv[:, 2048 + 64 * c:2048 + 64 * (c + 1)]
        wv = Wqkv[:, 2560 + 64 * c:2560 + 64 * (c + 1)]
        m["wkv_r"] = np.ascontiguousarray(
            np.concatenate([wk, wv], axis=1).reshape(16, 128, 128))
        m["wq_r"] = np.ascontiguousarray(
            Wqkv[:, 256 * c:256 * (c + 1)].reshape(16, 128, 256))
        m["sink"] = np.ascontiguousarray(s_in.reshape(NKV, QM)[c:c + 1, :])
        in_maps.append(m)
    return in_maps


def kernel(**inputs):
    in_maps = _host_prep(**inputs)
    nc = build_nc()
    res = run_bass_kernel_spmd(nc, in_maps, core_ids=list(range(NCORES)))
    out = np.concatenate(
        [np.asarray(r["out_d"], np.float32).T for r in res.results], axis=0)
    return out.reshape(1, T, D)
